# version 05
# baseline (speedup 1.0000x reference)
import math
import numpy as np
from contextlib import ExitStack

import concourse.bass as bass
import concourse.mybir as mybir
from concourse.bass_utils import run_bass_kernel_spmd

F32 = mybir.dt.float32
BF16 = mybir.dt.bfloat16
AF = mybir.ActivationFunctionType
ALU = mybir.AluOpType
AX = mybir.AxisListType

ENGINES = ("sync", "gpsimd", "tensor", "vector", "scalar")


class Cfg:
    def __init__(self, T=4096, D=1024, F=1024, FS=2048, H=16, NE=15,
                 TC=512, TC2=256):
        self.T, self.D, self.F, self.FS, self.H, self.NE = T, D, F, FS, H, NE
        self.TC, self.TC2 = TC, TC2
        self.KT = D // 128
        self.DH = 64
        self.NC = T // TC
        self.NC2 = T // TC2
        self.PB = F // 128
        self.SQ = FS // F
        self.NU = NE + self.SQ
        self.QC = TC
        self.NQC = T // self.QC
        self.KTT = T // 128
        self.CC = 16.0

    @property
    def slopes(self):
        n = self.H

        def pow2(n):
            start = 2.0 ** (-(2.0 ** (-(math.log2(n) - 3))))
            return [start * start ** i for i in range(n)]
        if math.log2(n).is_integer():
            return pow2(n)
        closest = 2 ** math.floor(math.log2(n))
        extra = pow2(2 * closest)[closest:]
        return pow2(closest) + extra[: n - closest]


class Sem:
    __slots__ = ("h", "n", "name")

    def __init__(self, es, nc, name):
        self.h = es.enter_context(nc.semaphore(name))
        self.n = 0
        self.name = name

    def bump(self, k=1):
        self.n += k
        return self.n


class Plan:
    def __init__(self):
        self.ops = {e: [] for e in ENGINES}

    def op(self, eng, fn, waits=(), inc=None, inck=1):
        if inc is not None:
            inc.bump(inck)
        self.ops[eng].append((fn, tuple(waits), inc, inck))

    def wait(self, eng, sem, val):
        self.ops[eng].append((None, ((sem, val),), None, 0))

    def emit(self, eng_name, eng):
        for fn, waits, inc, inck in self.ops[eng_name]:
            for (s, v) in waits:
                if v > 0:
                    eng.wait_ge(s.h, v)
            if fn is None:
                continue
            inst = fn(eng)
            if inc is not None:
                inst.then_inc(inc.h, inck)


def build_program(c: Cfg, dbg=False, stop_after=99):
    nc = bass.Bass("TRN2", target_bir_lowering=False)
    KIND = "ExternalOutput" if dbg else "Internal"
    es = ExitStack()
    DI = {}

    def din(name, shape, dtype=F32):
        DI[name] = nc.dram_tensor(name, list(shape), dtype,
                                  kind="ExternalInput").ap()
        return DI[name]

    d_xT = din("xT", (c.D, c.T))
    d_wq = din("wqT", (c.D, c.D))
    d_wk = din("wkT", (c.D, c.D))
    d_wv = din("wvT", (c.D, c.D))
    d_wo = din("woT", (c.D, c.D))
    d_battn = din("battn", (128, 4 * c.KT))
    d_gate = din("gateT", (c.D, c.NE))
    d_fc1w = din("fc1wT", (c.NE, c.D, 2 * c.F))
    d_fc1b = din("fc1b_sb", (128, (2 * c.F // 128) * c.NE))
    d_mult = din("mult_sb", (128, (c.F // 128) * c.NE))
    d_fc2w = din("fc2wT", (c.NE, c.F, c.D))
    d_b2cat = din("b2cat", (c.NE + 1, c.D))
    d_s1w = din("sfc1wT", (c.D, 2 * c.FS))
    d_s1b = din("sfc1b_sb", (128, 2 * c.FS // 128))
    d_smult = din("smult_sb", (128, c.FS // 128))
    d_s2w = din("sfc2wT", (c.FS, c.D))
    d_ones1 = din("ones1", (1, 128))
    d_onesc = din("onescol", (128, 1))
    d_idn = din("idn", (128, 128))
    d_oneh = din("onehot", (c.NE + 1, c.NE * 128))
    d_onesT = din("onesT", (1, c.T))
    d_cells = din("cellsv", (1, 8))
    d_alibi = din("alibi", (c.H, 128, c.QC))
    d_alibid = din("alibid", (c.H, 4, 128, c.QC))
    d_expc = din("expconst", (128, c.H * 32))

    d_out = nc.dram_tensor("outT", [c.D, c.T], F32, kind="ExternalOutput").ap()
    d_x1 = nc.dram_tensor("x1T", [c.D, c.T], F32, kind=KIND).ap()
    d_x2 = nc.dram_tensor("x2T", [c.D, c.T], F32, kind=KIND).ap()
    d_qh = nc.dram_tensor("qTh", [c.D, c.T], BF16, kind=KIND).ap()
    d_kh = nc.dram_tensor("kTh", [c.D, c.T], BF16, kind=KIND).ap()
    d_vh = nc.dram_tensor("vTh", [c.D, c.T], BF16, kind=KIND).ap()
    d_oh = nc.dram_tensor("oTh", [c.D, c.T], BF16, kind=KIND).ap()
    if dbg:
        d_dbg1 = nc.dram_tensor("dbg1", [128, c.KT * c.T], BF16,
                                kind="ExternalOutput").ap()
        d_dbg2 = nc.dram_tensor("dbg2", [128, c.KT * c.T], BF16,
                                kind="ExternalOutput").ap()
        d_dbg0 = nc.dram_tensor("dbg0", [128, c.KT * c.T], BF16,
                                kind="ExternalOutput").ap()

    SB_BASE = ((229376 - nc.sbuf_bytes_remaining + 95) // 32) * 32
    OFF = [SB_BASE]

    def sb(name, shape, dtype, at=None):
        esz = 2 if dtype == BF16 else 4
        nbytes = int(np.prod(shape[1:])) * esz
        if at is None:
            at = OFF[0]
            OFF[0] = at + ((nbytes + 31) // 32) * 32
        else:
            assert at + nbytes <= 229376, f"{name} overflows sbuf"
        return nc.alloc_sbuf_tensor_at(name, list(shape), dtype, offset=at)

    A0 = OFF[0]
    xln = sb("xln", (128, c.KT * c.T), BF16)
    _p3b = 2 * (c.KT * c.TC2 * 4) + 4 * (c.KT * c.TC2 * 2) + c.KT * c.D * 2 + 64
    OFF[0] = max(OFF[0], ((A0 + _p3b + 31) // 32) * 32)
    B0 = OFF[0]
    ybuf = sb("ybuf", (128, c.KT * c.T), BF16)
    _p4b = 6 * (2 * c.T) + ((c.KTT * (c.DH + 1) * 2 + 31) // 32) * 32 \
        + 3 * (2 * c.QC) + 4 * c.QC + 2 * c.QC + 4 * c.H * 32 + 4 * c.QC + 96
    OFF[0] = max(OFF[0], ((B0 + _p4b + 31) // 32) * 32)
    W1_0 = OFF[0]
    w1 = [sb(f"w1_{i}", (128, c.KT * 1024), BF16) for i in range(2)]
    W2_0 = OFF[0]
    HD = c.D // 2
    w2 = [sb(f"w2_{i}", (128, c.PB * HD), BF16) for i in range(2)]

    C0 = OFF[0]
    a_sb = sb("a_sb", (128, c.PB * c.TC), BF16)
    tmp_g = [sb(f"tmp_g{i}", (128, c.TC), F32) for i in range(2)]
    comb_sb = sb("comb_sb", (128, c.TC), F32)
    tmp_y = [sb(f"tmp_y{i}", (128, c.TC), BF16,
                at=C0 + c.PB * c.TC * 2 + i * c.TC * 4) for i in range(2)]
    C_EXT = C0 + c.PB * c.TC * 2 + 3 * c.TC * 4
    LNR0 = OFF[0]
    _lx = C0
    def _ln_sb(name, shape, dtype):
        nonlocal_holder[0] += 0
        t = sb(name, shape, dtype, at=_LX[0])
        esz = 2 if dtype == BF16 else 4
        _LX[0] += ((int(np.prod(shape[1:])) * esz + 31) // 32) * 32
        return t
    nonlocal_holder = [0]
    _LX = [C0]
    srow = _ln_sb("srow", (1, c.TC2), F32)
    qrow = _ln_sb("qrow", (1, c.TC2), F32)
    mrow = _ln_sb("mrow", (1, c.TC2), F32)
    m2row = _ln_sb("m2row", (1, c.TC2), F32)
    vrow = _ln_sb("vrow", (1, c.TC2), F32)
    rrow = _ln_sb("rrow", (1, c.TC2), F32)
    mbf = _ln_sb("mbf", (1, c.TC2), BF16)
    rbf = _ln_sb("rbf", (1, c.TC2), BF16)
    mbt = _ln_sb("mbt", (128, c.TC2), BF16)
    rbt = _ln_sb("rbt", (128, c.TC2), BF16)
    ubt = _ln_sb("ubt", (128, c.TC2), BF16)
    lg = _ln_sb("lg", (128, 16), F32)
    lgm = [_ln_sb(f"lgm{i}", (128, 16), F32) for i in range(3)]
    mx = [_ln_sb(f"mx{i}", (128, 1), F32) for i in range(4)]
    selt = _ln_sb("selt", (128, 16), F32)
    esel = _ln_sb("esel", (128, 16), F32)
    r3 = _ln_sb("r3", (128, 1), F32)
    combt = _ln_sb("combt", (128, 16), BF16)
    gate_b = _ln_sb("gate_b", (128, c.KT * c.NE), BF16)
    battn = _ln_sb("battn_s", (128, 4 * c.KT), F32)
    bk8 = _ln_sb("bk8", (128, c.KT), F32)
    cells = _ln_sb("cells", (1, 8), F32)
    assert _LX[0] <= C0 + 14336, f"LN set overflow {_LX[0] - C0}"
    th = sb("th", (128, c.QC), F32, at=C0)
    dgv = sb("dgv", (128, 4 * c.QC), F32, at=C0 + 2 * c.QC * 4 // 2)
    sbt = [sb(f"sbt{i}", (128, c.QC), F32,
              at=C0 + 2048 + 4 * c.QC * 4 + i * c.QC * 4) for i in range(2)]
    _ax = C_EXT
    fc1b = sb("fc1b", (128, (2 * c.F // 128) * c.NE), F32, at=_ax)
    _ax += (2 * c.F // 128) * c.NE * 4
    mult = sb("mult", (128, (c.F // 128) * c.NE), F32, at=_ax)
    _ax += (c.F // 128) * c.NE * 4
    s1b = sb("s1b", (128, 2 * c.FS // 128), F32, at=_ax)
    _ax += (2 * c.FS // 128) * 4
    smult = sb("smult", (128, c.FS // 128), F32, at=_ax)
    _ax += (c.FS // 128) * 4
    b2c = sb("b2c", (c.NE + 1, c.D), BF16, at=_ax)
    _ax += 2 * c.D
    oneh = sb("oneh_s", (c.NE + 1, c.NE * 128), BF16, at=_ax)
    _ax += 2 * c.NE * 128
    OFF[0] = max(OFF[0], ((_ax + 31) // 32) * 32)
    combT = sb("combT", (c.NE + 1, c.T), BF16)
    ones1 = sb("ones1_s", (1, 128), BF16)
    onesc = sb("onesc_s", (128, 1), BF16)
    idn = sb("idn_s", (128, 128), BF16)
    assert OFF[0] <= 229376, f"sbuf static overflow {OFF[0]}"

    tc2b4 = c.KT * c.TC2 * 4
    tc2b2 = c.KT * c.TC2 * 2

    def over(name, shape, dtype, at):
        esz = 2 if dtype == BF16 else 4
        assert at + int(np.prod(shape[1:])) * esz <= 229376, f"{name} overflow"
        return nc.alloc_sbuf_tensor_at(name, list(shape), dtype, offset=at)

    kt2 = (128, c.KT * c.TC2)
    xchA = over("xchA", kt2, F32, A0)
    yfA = over("yfA", kt2, F32, A0 + tc2b4)
    x1bA = over("x1bA", kt2, BF16, A0 + 2 * tc2b4)
    xsqA = over("xsqA", kt2, BF16, A0 + 2 * tc2b4 + tc2b2)
    xln2A = over("xln2A", kt2, BF16, A0 + 2 * tc2b4 + 2 * tc2b2)
    qstA = over("qstA", kt2, BF16, A0 + 2 * tc2b4 + 3 * tc2b2)
    wvA = over("wvA", (128, c.KT * c.D), BF16, A0 + 2 * tc2b4 + 4 * tc2b2)
    kstW = over("kstW", kt2, BF16, W2_0)
    vstW = over("vstW", kt2, BF16, W2_0 + tc2b2)
    xbW = over("xbW", kt2, BF16, W2_0)
    xqW = over("xqW", kt2, BF16, W2_0 + tc2b2)
    woW1 = over("woW1", (128, c.KT * c.D), BF16, W1_0)
    xchW1 = over("xchW1", kt2, F32, W1_0 + c.KT * c.D * 2)
    x1bW2 = over("x1bW2", kt2, BF16, W2_0)
    xsqW2 = over("xsqW2", kt2, BF16, W2_0 + tc2b2)
    ochW2 = over("ochW2", kt2, BF16, W2_0 + 2 * tc2b2)
    assert 3 * tc2b2 <= c.PB * HD * 2 * 2, "W2 overlay overflow"
    xchP7 = over("xchP7", kt2, F32, W1_0)
    yfP7 = over("yfP7", kt2, F32, W1_0 + tc2b4)
    hb = 2 * c.T
    qhB = [over(f"qhB{i}", (c.DH, c.T), BF16, B0 + i * hb) for i in range(2)]
    khB = [over(f"khB{i}", (c.DH, c.T), BF16, B0 + (2 + i) * hb) for i in range(2)]
    vhB = [over(f"vhB{i}", (c.DH, c.T), BF16, B0 + (4 + i) * hb) for i in range(2)]
    _b = B0 + 6 * hb
    vtB = over("vtB", (128, c.KTT * (c.DH + 1)), BF16, _b)
    _b += ((c.KTT * (c.DH + 1) * 2 + 31) // 32) * 32
    obuf = over("obuf", (c.DH, c.QC), BF16, _b); _b += 2 * c.QC
    obuf2 = over("obuf2", (c.DH, c.QC), BF16, _b); _b += 2 * c.QC
    rbsb = over("rbsb", (c.DH, c.QC), BF16, _b); _b += 2 * c.QC
    rrw = over("rrw", (1, c.QC), F32, _b); _b += 4 * c.QC
    rrb = over("rrb", (1, c.QC), BF16, _b); _b += 2 * c.QC
    expc = over("expc", (128, c.H * 32), F32, _b); _b += 4 * c.H * 32
    ptt = [over(f"ptt{i}", (128, c.QC), BF16, _b + i * 2 * c.QC)
           for i in range(2)]
    _b += 4 * c.QC
    assert _b <= OFF[0], "arena B overflow"

    pb = [es.enter_context(nc.psum_tensor(f"pb{i}", [128, 512], F32))
          for i in range(7)]
    pbt = es.enter_context(nc.psum_tensor("pbt", [128, 1024], BF16))

    S = {}
    for name in ("ld", "ldg", "st", "ost", "w1a", "w1b", "w2a", "w2b",
                 "xa", "xb", "xga", "xgb", "oc", "h0", "h1", "qst", "kst",
                 "vst", "thld",
                 "pe", "dve", "act", "gpc",
                 "b_sync", "b_gpsimd", "b_tensor", "b_vector", "b_scalar"):
        S[name] = Sem(es, nc, name)
    P = Plan()
    barno = [0]

    def barrier():
        barno[0] += 1
        n = barno[0]
        for e in ENGINES:
            if e == "sync":
                P.wait(e, S["st"], S["st"].n)
                P.wait(e, S["ost"], S["ost"].n)
                P.wait(e, S["ld"], S["ld"].n)
                for nm in ("qst", "kst", "vst", "xa", "xb", "oc",
                           "h0", "h1"):
                    P.wait(e, S[nm], S[nm].n)
            if e == "gpsimd":
                for nm in ("ldg", "w1a", "w1b", "w2a", "w2b", "thld",
                           "xga", "xgb"):
                    P.wait(e, S[nm], S[nm].n)
            P.op(e, lambda E: E.drain(), inc=S["b_" + e])
        for e in ENGINES:
            for o in ENGINES:
                if o != e:
                    P.wait(e, S["b_" + o], n)

    def dma(eng, out, in_, sem, waits=()):
        P.op(eng, (lambda o, i: (lambda E: E.dma_start(out=o, in_=i)))(out, in_),
             waits=waits, inc=sem, inck=16)

    def kt_slice(d_t, csl):
        return d_t.rearrange("(kt p) t -> p kt t", p=128)[:, :, csl]

    def kt3(sbuf_t, n_t):
        return sbuf_t.rearrange("p (kt t) -> p kt t", t=n_t)

    dma("gpsimd", combT[c.NE:c.NE + 1, :], d_onesT, S["ldg"])
    dma("gpsimd", ones1[:, :], d_ones1, S["ldg"])
    dma("gpsimd", onesc[:, :], d_onesc, S["ldg"])
    dma("gpsimd", idn[:, :], d_idn, S["ldg"])

    def load_ln_tables(want_gate, want_battn):
        dma("gpsimd", cells[:, :], d_cells, S["ldg"])
        if want_gate:
            dma("gpsimd", gate_b[:, :],
                d_gate.rearrange("(kt p) e -> p kt e", p=128), S["ldg"])
        if want_battn:
            dma("gpsimd", battn[:, :], d_battn, S["ldg"])
        n = S["ldg"].n
        for e in ("scalar", "vector", "tensor"):
            P.wait(e, S["ldg"], n)
        if want_battn:
            P.op("vector", lambda E: E.tensor_scalar(
                bk8[:, :], battn[:, c.KT:2 * c.KT], 0.125, None, ALU.mult),
                inc=S["dve"])
            return S["dve"].n
        return 0
    barrier()

    def emit_ln(src_bf, src_sq, dst, dst_stride, dst_t0, router,
                pe_ready=0, dve_war=0, stats_free=0, pb6_free=0):
        for kt in range(c.KT):
            P.op("tensor", (lambda k=kt: (lambda E: E.matmul(
                pb[4][0:1, 0:c.TC2], onesc[:, :],
                src_bf[:, k * c.TC2:(k + 1) * c.TC2],
                start=(k == 0), stop=(k == c.KT - 1))))(),
                waits=([(S["pe"], pe_ready), (S["dve"], stats_free)]
                       if kt == 0 else ()),
                inc=(S["pe"] if kt == c.KT - 1 else None))
        pe_sum = S["pe"].n
        for kt in range(c.KT):
            P.op("tensor", (lambda k=kt: (lambda E: E.matmul(
                pb[5][0:1, 0:c.TC2], onesc[:, :],
                src_sq[:, k * c.TC2:(k + 1) * c.TC2],
                start=(k == 0), stop=(k == c.KT - 1))))(),
                waits=([(S["dve"], stats_free)] if kt == 0 else ()),
                inc=(S["pe"] if kt == c.KT - 1 else None))
        pe_sq = S["pe"].n
        P.op("vector", lambda E: E.tensor_scalar(
            mrow[0:1, :], pb[4][0:1, 0:c.TC2], 1.0 / c.D, None, ALU.mult),
            waits=[(S["pe"], pe_sum)], inc=S["dve"])
        d_m = S["dve"].n
        P.op("vector", lambda E: E.tensor_scalar(
            qrow[0:1, :], pb[5][0:1, 0:c.TC2], 1.0 / c.D, None, ALU.mult),
            waits=[(S["pe"], pe_sq)], inc=S["dve"])
        stats_rd = S["dve"].n
        P.op("vector", lambda E: E.tensor_tensor(
            m2row[0:1, :], mrow[0:1, :], mrow[0:1, :], ALU.mult),
            waits=[(S["dve"], S["dve"].n)], inc=S["dve"])
        P.op("vector", lambda E: E.tensor_tensor(
            vrow[0:1, :], qrow[0:1, :], m2row[0:1, :], ALU.subtract),
            waits=[(S["dve"], S["dve"].n)], inc=S["dve"])
        d_v = S["dve"].n
        P.op("scalar", lambda E: E.activation(
            vrow[0:1, :], vrow[0:1, :], AF.Ln, bias=cells[0:1, 0:1]),
            waits=[(S["dve"], d_v)], inc=S["act"])
        P.op("scalar", lambda E: E.activation(
            rrow[0:1, :], vrow[0:1, :], AF.Exp, scale=-0.5),
            waits=[(S["act"], S["act"].n)], inc=S["act"])
        a_r = S["act"].n
        P.op("vector", lambda E: E.tensor_copy(mbf[0:1, :], mrow[0:1, :]),
             waits=[(S["dve"], d_m)], inc=S["dve"])
        d_mb = S["dve"].n
        P.op("vector", lambda E: E.tensor_copy(rbf[0:1, :], rrow[0:1, :]),
             waits=[(S["act"], a_r)], inc=S["dve"])
        d_rb = S["dve"].n
        P.op("tensor", lambda E: E.matmul(
            pb[2][:, 0:c.TC2], ones1[:, :], mbf[0:1, :], start=True, stop=True),
            waits=[(S["dve"], d_mb)], inc=S["pe"])
        pe_mb = S["pe"].n
        P.op("tensor", lambda E: E.matmul(
            pb[3][:, 0:c.TC2], ones1[:, :], rbf[0:1, :], start=True, stop=True),
            waits=[(S["dve"], d_rb)], inc=S["pe"])
        pe_rb = S["pe"].n
        P.op("vector", lambda E: E.tensor_copy(mbt[:, :], pb[2][:, 0:c.TC2]),
             waits=[(S["pe"], pe_mb)], inc=S["dve"])
        P.op("vector", lambda E: E.tensor_copy(rbt[:, :], pb[3][:, 0:c.TC2]),
             waits=[(S["pe"], pe_rb)], inc=S["dve"])
        d_rbt = S["dve"].n
        for kt in range(c.KT):
            P.op("vector", (lambda k=kt: (lambda E: E.tensor_tensor(
                ubt[:, :], src_bf[:, k * c.TC2:(k + 1) * c.TC2], mbt[:, :],
                ALU.subtract)))(),
                waits=[(S["dve"], max(d_rbt, S["dve"].n))],
                inc=S["dve"])
            d_u = S["dve"].n
            P.op("vector", (lambda k=kt: (lambda E: E.tensor_tensor(
                dst[:, k * dst_stride + dst_t0: k * dst_stride + dst_t0 + c.TC2],
                ubt[:, :], rbt[:, :], ALU.mult)))(),
                waits=[(S["dve"], d_u)] +
                      ([(S["pe"], dve_war)] if (dve_war and kt == 0) else []),
                inc=S["dve"])
        dve_ln = S["dve"].n
        if not router:
            return dve_ln, stats_rd, pb6_free
        tp_free = 0
        for tt in range(c.TC2 // 128):
            tg = dst_t0 + tt * 128
            for kt in range(c.KT):
                P.op("tensor", (lambda k=kt, tg=tg: (lambda E: E.matmul(
                    pb[6][:, 0:c.NE],
                    dst[:, k * dst_stride + tg: k * dst_stride + tg + 128],
                    gate_b[:, k * c.NE:(k + 1) * c.NE],
                    start=(k == 0), stop=(k == c.KT - 1))))(),
                    waits=([(S["dve"], dve_ln), (S["dve"], pb6_free)]
                           if kt == 0 else ()),
                    inc=(S["pe"] if kt == c.KT - 1 else None))
            pe_lg = S["pe"].n
            P.op("vector", lambda E: E.tensor_copy(lg[:, 0:c.NE],
                                                   pb[6][:, 0:c.NE]),
                 waits=[(S["pe"], pe_lg)], inc=S["dve"])
            pb6_free = S["dve"].n
            P.op("vector", lambda E: E.reduce_max(mx[0][:, :], lg[:, 0:c.NE],
                                                  axis=AX.X),
                 waits=[(S["dve"], S["dve"].n)], inc=S["dve"])
            P.op("vector", lambda E: E.tensor_scalar(
                lgm[1][:, 0:c.NE], lg[:, 0:c.NE], mx[0][:, 0:1], None, ALU.is_ge),
                waits=[(S["dve"], S["dve"].n)], inc=S["dve"])
            P.op("vector", lambda E: E.scalar_tensor_tensor(
                lgm[0][:, 0:c.NE], lgm[1][:, 0:c.NE], -1e30, lg[:, 0:c.NE],
                ALU.mult, ALU.add),
                waits=[(S["dve"], S["dve"].n)], inc=S["dve"])
            P.op("vector", lambda E: E.reduce_max(mx[1][:, :], lgm[0][:, 0:c.NE],
                                                  axis=AX.X),
                 waits=[(S["dve"], S["dve"].n)], inc=S["dve"])
            P.op("vector", lambda E: E.tensor_scalar(
                lgm[2][:, 0:c.NE], lgm[0][:, 0:c.NE], mx[1][:, 0:1], None,
                ALU.is_ge),
                waits=[(S["dve"], S["dve"].n)], inc=S["dve"])
            P.op("vector", lambda E: E.scalar_tensor_tensor(
                lgm[0][:, 0:c.NE], lgm[2][:, 0:c.NE], -1e30, lgm[0][:, 0:c.NE],
                ALU.mult, ALU.add),
                waits=[(S["dve"], S["dve"].n)], inc=S["dve"])
            P.op("vector", lambda E: E.reduce_max(mx[1][:, :], lgm[0][:, 0:c.NE],
                                                  axis=AX.X),
                 waits=[(S["dve"], S["dve"].n)], inc=S["dve"])
            d_m3 = S["dve"].n
            P.op("vector", lambda E: E.tensor_scalar(
                selt[:, 0:c.NE], lg[:, 0:c.NE], mx[1][:, 0:1], None, ALU.is_ge),
                waits=[(S["dve"], d_m3)], inc=S["dve"])
            d_sel = S["dve"].n
            P.op("vector", lambda E: E.tensor_scalar(
                mx[2][:, :], mx[0][:, :], -1.0, None, ALU.mult),
                waits=[(S["dve"], d_m3)], inc=S["dve"])
            d_m1n = S["dve"].n
            P.op("scalar", lambda E: E.activation(
                lgm[1][:, 0:c.NE], lg[:, 0:c.NE], AF.Exp, bias=mx[2][:, 0:1]),
                waits=[(S["dve"], d_m1n)], inc=S["act"])
            a_e = S["act"].n
            P.op("vector", lambda E: E.tensor_tensor(
                esel[:, 0:c.NE], lgm[1][:, 0:c.NE], selt[:, 0:c.NE], ALU.mult),
                waits=[(S["act"], a_e), (S["dve"], d_sel)], inc=S["dve"])
            P.op("vector", lambda E: E.reduce_sum(mx[3][:, :], esel[:, 0:c.NE],
                                                  axis=AX.X),
                 waits=[(S["dve"], S["dve"].n)], inc=S["dve"])
            P.op("vector", lambda E: E.reciprocal(r3[:, :], mx[3][:, :]),
                 waits=[(S["dve"], S["dve"].n)], inc=S["dve"])
            P.op("vector", lambda E: E.tensor_scalar(
                combt[:, 0:c.NE], esel[:, 0:c.NE], r3[:, 0:1], None, ALU.mult),
                waits=[(S["dve"], S["dve"].n)], inc=S["dve"])
            d_cb = S["dve"].n
            P.op("tensor", lambda E: E.transpose(
                pbt[0:c.NE, 0:128], combt[:, 0:c.NE], idn[:, :]),
                waits=[(S["dve"], d_cb), (S["dve"], tp_free)], inc=S["pe"])
            pe_tp = S["pe"].n
            P.op("vector", (lambda tg=tg: (lambda E: E.tensor_copy(
                combT[0:c.NE, tg:tg + 128], pbt[0:c.NE, 0:128])))(),
                waits=[(S["pe"], pe_tp)], inc=S["dve"])
            tp_free = S["dve"].n
        return S["dve"].n, stats_rd, pb6_free

    load_ln_tables(want_gate=True, want_battn=False)
    prev_done = 0
    stats_fr = 0
    pb6_fr = 0
    for ci in range(c.NC2):
        t0 = ci * c.TC2
        csl = slice(t0, t0 + c.TC2)
        ldsem = S["xga"] if ci % 2 == 0 else S["xgb"]
        dma("gpsimd", kt3(xbW, c.TC2)[:, :, :], kt_slice(d_xT, csl), ldsem,
            waits=[(S["dve"], prev_done)])
        xv = ldsem.n
        P.op("scalar", lambda E: E.activation(xqW[:, :], xbW[:, :], AF.Square),
             waits=[(S[ldsem.name], xv), (S["dve"], prev_done)], inc=S["act"])
        a_sq = S["act"].n
        P.wait("tensor", S[ldsem.name], xv)
        P.wait("tensor", S["act"], a_sq)
        prev_done, stats_fr, pb6_fr = emit_ln(
            xbW, xqW, xln, c.T, t0, router=True,
            stats_free=stats_fr, pb6_free=pb6_fr)
    barrier()
    if dbg:
        dma("sync", d_dbg0, xln[:, :], S["st"])
        for _e in ENGINES:
            P.wait(_e, S["st"], S["st"].n)

    FT = c.PB

    def emit_ffn():
        def w1_src(u, h):
            if u < c.NE:
                base = d_fc1w[u].rearrange("(kt p) f -> p kt f", p=128)
                xo, go = 4 * h * 128, (c.PB + 4 * h) * 128
            else:
                sq = u - c.NE
                base = d_s1w.rearrange("(kt p) f -> p kt f", p=128)
                xo = (sq * c.PB + 4 * h) * 128
                go = ((c.FS // 128) + sq * c.PB + 4 * h) * 128
            return base[:, :, xo:xo + 512], base[:, :, go:go + 512]

        def w2_src(u, j):
            if u < c.NE:
                return d_fc2w[u].rearrange("(kt p) d -> p kt d", p=128)[
                    :, :, j * HD:(j + 1) * HD]
            sq = u - c.NE
            return d_s2w.rearrange("(kt p) d -> p kt d", p=128)[
                :, sq * c.PB:(sq + 1) * c.PB, j * HD:(j + 1) * HD]

        def bias_cols(u, i):
            if u < c.NE:
                return (fc1b[:, i * c.NE + u: i * c.NE + u + 1],
                        fc1b[:, (c.PB + i) * c.NE + u:(c.PB + i) * c.NE + u + 1])
            sq = u - c.NE
            return (s1b[:, sq * c.PB + i: sq * c.PB + i + 1],
                    s1b[:, c.FS // 128 + sq * c.PB + i:
                        c.FS // 128 + sq * c.PB + i + 1])

        def mult_col(u, kt):
            if u < c.NE:
                return mult[:, kt * c.NE + u: kt * c.NE + u + 1]
            sq = u - c.NE
            return smult[:, sq * c.PB + kt: sq * c.PB + kt + 1]

        dma("gpsimd", b2c[:, :], d_b2cat, S["ldg"])
        dma("gpsimd", oneh[:, :], d_oneh, S["ldg"])
        dma("gpsimd", fc1b[:, :], d_fc1b, S["ldg"])
        dma("gpsimd", mult[:, :], d_mult, S["ldg"])
        dma("gpsimd", s1b[:, :], d_s1b, S["ldg"])
        dma("gpsimd", smult[:, :], d_smult, S["ldg"])
        ldg_ok = S["ldg"].n
        P.wait("scalar", S["ldg"], ldg_ok)
        P.wait("vector", S["ldg"], ldg_ok)
        P.wait("tensor", S["ldg"], ldg_ok)
        w1sem = [S["w1a"], S["w1b"]]
        w2sem = [S["w2a"], S["w2b"]]
        w1_free = [S["pe"].n, S["pe"].n]
        w2_free = [S["pe"].n, S["pe"].n]
        w2_rdy = [0, 0]
        a_war = S["pe"].n
        comb_war = 0
        comb_bank_free = 0
        comb_rdy = 0
        pxh, pg, py = [pb[0], pb[1]], [pb[2], pb[3]], [pb[4], pb[5]]
        pxh_free = [0, 0]
        pg_free_act = [0, 0]
        py_free = [0, 0]
        tmpg_war = [0, 0]
        tmpy_rd = [0, 0]

        for u in range(c.NU):
            for h in range(2):
                xs, gs = w1_src(u, h)
                sem = w1sem[h]
                wdst = kt3(w1[h], 1024)
                dma("gpsimd", wdst[:, :, 0:512], xs, sem,
                    waits=[(S["pe"], w1_free[h])])
                dma("gpsimd", wdst[:, :, 512:1024], gs, sem)
            for j in range(2):
                sem = w2sem[j]
                dma("gpsimd", kt3(w2[j], HD)[:, :, :], w2_src(u, j), sem,
                    waits=[(S["pe"], w2_free[j])])
                wv2 = sem.n
                for kt in range(FT):
                    P.op("vector", (lambda j=j, kt=kt, u=u: (lambda E:
                         E.tensor_scalar(
                             w2[j][:, kt * HD:(kt + 1) * HD],
                             w2[j][:, kt * HD:(kt + 1) * HD],
                             mult_col(u, kt), None, ALU.mult)))(),
                        waits=([(S[sem.name], wv2)] if kt == 0 else
                               [(S["dve"], S["dve"].n)]),
                        inc=S["dve"])
                w2_rdy[j] = S["dve"].n
            w1_lastuse = [None, None]
            w2_lastuse = [None, None]
            for ch in range(c.NC):
                t0 = ch * c.TC
                if u < c.NE:
                    P.op("tensor", (lambda u=u, t0=t0: (lambda E: E.matmul(
                        pb[6][:, 0:c.TC], oneh[:, u * 128:(u + 1) * 128],
                        combT[0:c.NE + 1, t0:t0 + c.TC],
                        start=True, stop=True)))(),
                        waits=[(S["dve"], comb_bank_free),
                               (S["ldg"], ldg_ok)], inc=S["pe"])
                    pe_cb = S["pe"].n
                    P.op("vector", lambda E: E.tensor_copy(
                        comb_sb[:, :], pb[6][:, 0:c.TC]),
                        waits=[(S["pe"], pe_cb), (S["dve"], comb_war)],
                        inc=S["dve"])
                    comb_bank_free = S["dve"].n
                    comb_rdy = S["dve"].n
                for i in range(c.PB):
                    b = i % 2
                    h = i // 4
                    wsem = w1sem[h]
                    wv = wsem.n
                    wcol = (i % 4) * 128
                    for k in range(c.KT):
                        P.op("tensor", (lambda h=h, k=k, wcol=wcol, t0=t0, b=b:
                             (lambda E: E.matmul(
                                 pxh[b][:, 0:c.TC],
                                 w1[h][:, k * 1024 + wcol: k * 1024 + wcol + 128],
                                 xln[:, k * c.T + t0: k * c.T + t0 + c.TC],
                                 start=(k == 0), stop=(k == c.KT - 1))))(),
                            waits=([(S[wsem.name], wv),
                                    (S["dve"], pxh_free[b])] if k == 0 else ()))
                    for k in range(c.KT):
                        P.op("tensor", (lambda h=h, k=k, wcol=wcol, t0=t0, b=b:
                             (lambda E: E.matmul(
                                 pg[b][:, 0:c.TC],
                                 w1[h][:, k * 1024 + 512 + wcol:
                                       k * 1024 + 512 + wcol + 128],
                                 xln[:, k * c.T + t0: k * c.T + t0 + c.TC],
                                 start=(k == 0), stop=(k == c.KT - 1))))(),
                            waits=([(S["act"], pg_free_act[b])] if k == 0 else ()),
                            inc=(S["pe"] if k == c.KT - 1 else None))
                    pe_pair = S["pe"].n
                    if ch == c.NC - 1 and i % 4 == 3:
                        w1_lastuse[h] = pe_pair
                    bxh, bg = bias_cols(u, i)
                    P.op("scalar", (lambda b=b, bg=bg: (lambda E: E.activation(
                        tmp_g[b][:, :], pg[b][:, 0:c.TC], AF.Gelu, bias=bg)))(),
                        waits=[(S["pe"], pe_pair), (S["dve"], tmpg_war[b]),
                               (S["dve"], tmpy_rd[b])],
                        inc=S["act"])
                    pg_free_act[b] = S["act"].n
                    g_done = S["act"].n
                    P.op("vector", (lambda b=b, i=i, bxh=bxh: (lambda E:
                         E.scalar_tensor_tensor(
                             a_sb[:, i * c.TC:(i + 1) * c.TC],
                             pxh[b][:, 0:c.TC], bxh, tmp_g[b][:, :],
                             ALU.add, ALU.mult)))(),
                        waits=[(S["act"], g_done), (S["pe"], a_war)],
                        inc=S["dve"])
                    pxh_free[b] = S["dve"].n
                    tmpg_war[b] = S["dve"].n
                a_all = S["dve"].n
                for m in range(c.KT):
                    b = m % 2
                    j = m // (c.KT // 2)
                    wsem = w2sem[j]
                    wv = wsem.n
                    last_u = (u == c.NU - 1)
                    for k in range(FT):
                        P.op("tensor", (lambda j=j, k=k, m=m, b=b, last_u=last_u:
                             (lambda E: E.matmul(
                                 py[b][:, 0:c.TC],
                                 w2[j][:, k * HD + (m % (c.KT // 2)) * 128:
                                       k * HD + (m % (c.KT // 2)) * 128 + 128],
                                 a_sb[:, k * c.TC:(k + 1) * c.TC],
                                 start=(k == 0),
                                 stop=(k == FT - 1 and not last_u))))(),
                            waits=([(S[wsem.name], wv),
                                    (S["dve"], max(a_all, w2_rdy[j])),
                                    (S["dve"], py_free[b])] if k == 0 else ()),
                            inc=(S["pe"] if (k == FT - 1 and not last_u)
                                 else None))
                    if last_u:
                        P.op("tensor", (lambda m=m, b=b, t0=t0: (lambda E:
                             E.matmul(
                                 py[b][:, 0:c.TC],
                                 b2c[0:c.NE + 1, m * 128:(m + 1) * 128],
                                 combT[0:c.NE + 1, t0:t0 + c.TC],
                                 start=False, stop=True)))(),
                            inc=S["pe"])
                    pe_y = S["pe"].n
                    if ch == c.NC - 1 and (m % (c.KT // 2)) == c.KT // 2 - 1:
                        w2_lastuse[j] = pe_y
                    ysl = ybuf[:, m * c.T + t0: m * c.T + t0 + c.TC]
                    tbuf = tmp_y[m % 2]
                    if u == 0:
                        P.op("vector", (lambda ysl=ysl, b=b: (lambda E:
                             E.tensor_tensor(
                                 ysl, py[b][:, 0:c.TC], comb_sb[:, :],
                                 ALU.mult)))(),
                            waits=[(S["pe"], pe_y), (S["dve"], comb_rdy)],
                            inc=S["dve"])
                        py_free[b] = S["dve"].n
                        comb_war = S["dve"].n
                    else:
                        if u < c.NE:
                            P.op("vector", (lambda tbuf=tbuf, b=b: (lambda E:
                                 E.tensor_tensor(
                                     tbuf[:, :], py[b][:, 0:c.TC],
                                     comb_sb[:, :], ALU.mult)))(),
                                waits=[(S["pe"], pe_y), (S["dve"], comb_rdy),
                                       (S["dve"], tmpy_rd[m % 2])],
                                inc=S["dve"])
                            comb_war = S["dve"].n
                        else:
                            P.op("vector", (lambda tbuf=tbuf, b=b: (lambda E:
                                 E.tensor_copy(tbuf[:, :], py[b][:, 0:c.TC])))(),
                                waits=[(S["pe"], pe_y),
                                       (S["dve"], tmpy_rd[m % 2])],
                                inc=S["dve"])
                        py_free[b] = S["dve"].n
                        d_t = S["dve"].n
                        P.op("vector", (lambda ysl=ysl, tbuf=tbuf: (lambda E:
                             E.tensor_tensor(ysl, ysl, tbuf[:, :], ALU.add)))(),
                            waits=[(S["dve"], d_t)], inc=S["dve"])
                        tmpy_rd[m % 2] = S["dve"].n
                a_war = pe_y
            w1_free = [w1_lastuse[0], w1_lastuse[1]]
            w2_free = [w2_lastuse[0], w2_lastuse[1]]

    emit_ffn()
    barrier()
    if dbg:
        dma("sync", d_dbg1, xln[:, :], S["st"])
        dma("sync", d_dbg2, ybuf[:, :], S["st"])
        for _e in ENGINES:
            P.wait(_e, S["st"], S["st"].n)
    if stop_after <= 2:
        P.wait("sync", S["st"], S["st"].n)
        with nc.Block() as block:
            block.sync(lambda E: P.emit("sync", E))
            block.gpsimd(lambda E: P.emit("gpsimd", E))
            block.tensor(lambda E: P.emit("tensor", E))
            block.vector(lambda E: P.emit("vector", E))
            block.scalar(lambda E: P.emit("scalar", E))
        es.close()
        return nc, list(DI.keys())

    dma("gpsimd", w1[0][:, :], d_wq.rearrange("(kt p) d -> p kt d", p=128),
        S["w1a"])
    dma("gpsimd", w1[1][:, :], d_wk.rearrange("(kt p) d -> p kt d", p=128),
        S["w1b"])
    dma("gpsimd", wvA[:, :], d_wv.rearrange("(kt p) d -> p kt d", p=128),
        S["w2a"])
    wq_ok, wk_ok, wv_ok = S["w1a"].n, S["w1b"].n, S["w2a"].n
    bk8_ok = load_ln_tables(want_gate=False, want_battn=True)
    x_rd = 0
    qkv_prev = 0
    ln_prev = 0
    xch_act = 0
    st_prev = 0
    stats_fr = 0
    for ci in range(c.NC2):
        t0 = ci * c.TC2
        csl = slice(t0, t0 + c.TC2)
        ldsem = S["xa"] if ci % 2 == 0 else S["xb"]
        dma("sync", kt3(xchA, c.TC2)[:, :, :], kt_slice(d_xT, csl), ldsem,
            waits=[(S["dve"], x_rd), (S["act"], xch_act),
                   (S["st"], st_prev)])
        xv = ldsem.n
        P.op("scalar", (lambda t0=t0: (lambda E: E.activation(
            kt3(yfA, c.TC2)[:, :, :], kt3(ybuf, c.T)[:, :, t0:t0 + c.TC2],
            AF.Identity, scale=0.5)))(),
            waits=[(S["dve"], x_rd)], inc=S["act"])
        a_yf = S["act"].n
        P.op("vector", lambda E: E.tensor_tensor(
            xchA[:, :], yfA[:, :], xchA[:, :], ALU.add),
            waits=[(S["act"], a_yf), (S[ldsem.name], xv)], inc=S["dve"])
        dvx1 = S["dve"].n
        x_rd = dvx1
        dma("sync", kt_slice(d_x1, csl), kt3(xchA, c.TC2)[:, :, :], S["st"],
            waits=[(S["dve"], dvx1)])
        st_prev = S["st"].n
        P.op("scalar", lambda E: E.activation(x1bA[:, :], xchA[:, :], AF.Copy),
             waits=[(S["dve"], dvx1), (S["dve"], ln_prev)], inc=S["act"])
        a_b = S["act"].n
        xch_act = a_b
        P.op("scalar", lambda E: E.activation(xsqA[:, :], x1bA[:, :], AF.Square),
             waits=[(S["act"], a_b)], inc=S["act"])
        a_q = S["act"].n
        P.wait("tensor", S["act"], a_q)
        ln_prev, stats_fr, _ = emit_ln(x1bA, xsqA, xln2A, c.TC2, 0,
                                       router=False, dve_war=qkv_prev,
                                       stats_free=stats_fr)
        pbfree = [0, 0]
        for mat, wbuf, wok, stg, stsem, dmat in (
                ("q", w1[0], wq_ok, qstA, S["qst"], d_qh),
                ("k", w1[1], wk_ok, kstW, S["kst"], d_kh),
                ("v", wvA, wv_ok, vstW, S["vst"], d_vh)):
            bcol = {"q": 0, "k": 1, "v": 2}[mat]
            for m in range(c.KT):
                b = m % 2
                for k in range(c.KT):
                    w_ = ([(S["dve"], ln_prev),
                           ({"q": S["w1a"], "k": S["w1b"], "v": S["w2a"]}[mat],
                            wok),
                           (S["act"], pbfree[b])] if k == 0 else ())
                    P.op("tensor", (lambda wbuf=wbuf, k=k, m=m, b=b: (lambda E:
                         E.matmul(
                             pb[b][:, 0:c.TC2],
                             wbuf[:, k * c.D + m * 128: k * c.D + m * 128 + 128],
                             xln2A[:, k * c.TC2:(k + 1) * c.TC2],
                             start=(k == 0), stop=(k == c.KT - 1))))(),
                        waits=w_, inc=(S["pe"] if k == c.KT - 1 else None))
                pe_m = S["pe"].n
                qkv_prev = pe_m
                if mat == "k":
                    bias_ap, sc = bk8[:, m:m + 1], 0.125
                else:
                    bias_ap, sc = battn[:, bcol * c.KT + m:
                                        bcol * c.KT + m + 1], 1.0
                P.op("scalar", (lambda stg=stg, m=m, bias_ap=bias_ap, sc=sc,
                     b=b: (lambda E: E.activation(
                         stg[:, m * c.TC2:(m + 1) * c.TC2], pb[b][:, 0:c.TC2],
                         AF.Identity, bias=bias_ap, scale=sc)))(),
                    waits=[(S["pe"], pe_m), (S[stsem.name], stsem.n)],
                    inc=S["act"])
                pbfree[b] = S["act"].n
            a_stg = S["act"].n
            dma("sync", kt_slice(dmat, csl), kt3(stg, c.TC2)[:, :, :], stsem,
                waits=[(S["act"], a_stg)])
    barrier()

    dma("sync", expc[:, :], d_expc, S["ld"])
    P.wait("scalar", S["ld"], S["ld"].n)
    setlast = [0, 0]
    vt_rd = 0
    th_rd = 0
    o_done = 0
    for h in range(c.H):
        hs = h % 2
        hsem = S["h0"] if hs == 0 else S["h1"]
        dma("sync", qhB[hs][:, :], d_qh[h * c.DH:(h + 1) * c.DH, :], hsem,
            waits=[(S["pe"], setlast[hs])])
        dma("sync", khB[hs][:, :], d_kh[h * c.DH:(h + 1) * c.DH, :], hsem)
        dma("sync", vhB[hs][:, :], d_vh[h * c.DH:(h + 1) * c.DH, :], hsem)
        hv = hsem.n
        dma("gpsimd", th[:, :], d_alibi[h], S["thld"],
            waits=[(S["dve"], th_rd)])
        dma("gpsimd", kt3(dgv, c.QC)[:, 0:4, :],
            d_alibid[h].rearrange("v p q -> p v q"), S["thld"])
        thv = S["thld"].n
        if h == 0:
            P.op("gpsimd", lambda E: E.memset(
                vtB.rearrange("p (kt e) -> p kt e", e=c.DH + 1)[
                    :, :, c.DH:c.DH + 1], 1.0), inc=S["gpc"])
            gset = S["gpc"].n
        tp_free = 0
        for kt in range(c.KTT):
            P.op("tensor", (lambda kt=kt, hs=hs: (lambda E: E.transpose(
                pbt[:, 0:c.DH], vhB[hs][:, kt * 128:(kt + 1) * 128],
                idn[0:c.DH, 0:c.DH])))(),
                waits=([(S[hsem.name], hv), (S["pe"], vt_rd)] if kt == 0
                       else [(S["dve"], tp_free)]),
                inc=S["pe"])
            pe_t = S["pe"].n
            P.op("vector", (lambda kt=kt: (lambda E: E.tensor_copy(
                vtB[:, kt * (c.DH + 1): kt * (c.DH + 1) + c.DH],
                pbt[:, 0:c.DH])))(),
                waits=[(S["pe"], pe_t)] +
                      ([(S["gpc"], gset)] if (h == 0 and kt == 0) else []),
                inc=S["dve"])
            tp_free = S["dve"].n
        d_vt = S["dve"].n
        sb_free = [0, 0]
        pt_free = [0, 0]
        ps_free = [0, 0]
        for qc in range(c.NQC):
            qt0 = qc * c.QC
            nkt = (qt0 + c.QC) // 128
            for kt in range(nkt):
                b = kt % 2
                P.op("tensor", (lambda kt=kt, qt0=qt0, b=b, hs=hs: (lambda E:
                     E.matmul(
                         pb[b][:, 0:c.QC],
                         khB[hs][:, kt * 128:(kt + 1) * 128],
                         qhB[hs][:, qt0:qt0 + c.QC],
                         start=True, stop=True)))(),
                    waits=[(S["dve"], ps_free[b])] +
                          ([(S[hsem.name], hv)] if (qc == 0 and kt < 2) else []),
                    inc=S["pe"])
                pe_s = S["pe"].n
                if kt * 128 >= qt0:
                    v = (kt * 128 - qt0) // 128
                    btile = dgv[:, v * c.QC:(v + 1) * c.QC]
                else:
                    btile = th[:, :]
                P.op("vector", (lambda b=b, btile=btile: (lambda E:
                     E.tensor_tensor(
                         sbt[b][:, :], pb[b][:, 0:c.QC], btile, ALU.add)))(),
                    waits=[(S["pe"], pe_s), (S["act"], sb_free[b])] +
                          ([(S["thld"], thv)] if (qc == 0 and kt < 2) else []),
                    inc=S["dve"])
                ps_free[b] = S["dve"].n
                d_sb = S["dve"].n
                oi = (kt * 128 - qt0 + 3584) // 128
                eccol = expc[:, h * 32 + oi: h * 32 + oi + 1]
                P.op("scalar", (lambda b=b, eccol=eccol: (lambda E:
                     E.activation(
                         ptt[b][:, :], sbt[b][:, :], AF.Exp, bias=eccol)))(),
                    waits=[(S["dve"], d_sb), (S["pe"], pt_free[b])],
                    inc=S["act"])
                sb_free[b] = S["act"].n
                a_pt = S["act"].n
                P.op("tensor", (lambda kt=kt, b=b, nkt=nkt: (lambda E:
                     E.matmul(
                         pb[4][0:c.DH + 1, 0:c.QC],
                         vtB[:, kt * (c.DH + 1):(kt + 1) * (c.DH + 1)],
                         ptt[b][:, :], start=(kt == 0),
                         stop=(kt == nkt - 1))))(),
                    waits=[(S["act"], a_pt)] +
                          ([(S["dve"], d_vt)] if kt == 0 else []) +
                          ([(S["dve"], o_done)] if kt == 0 else []),
                    inc=(S["pe"] if kt == nkt - 1 else None))
                pt_free[b] = S["pe"].n
            pe_o = S["pe"].n
            P.op("vector", lambda E: E.reciprocal(
                rrw[0:1, :], pb[4][c.DH:c.DH + 1, 0:c.QC]),
                waits=[(S["pe"], pe_o)], inc=S["dve"])
            P.op("vector", lambda E: E.tensor_copy(rrb[0:1, :], rrw[0:1, :]),
                 waits=[(S["dve"], S["dve"].n)], inc=S["dve"])
            d_rr = S["dve"].n
            P.op("tensor", lambda E: E.matmul(
                pb[5][0:c.DH, 0:c.QC], ones1[0:1, 0:c.DH], rrb[0:1, :],
                start=True, stop=True),
                waits=[(S["dve"], d_rr)], inc=S["pe"])
            pe_rb = S["pe"].n
            P.op("scalar", lambda E: E.activation(
                obuf[:, :], pb[4][0:c.DH, 0:c.QC], AF.Copy),
                waits=[(S["pe"], pe_o)], inc=S["act"])
            a_ob = S["act"].n
            P.op("vector", lambda E: E.tensor_copy(
                rbsb[:, :], pb[5][0:c.DH, 0:c.QC]),
                waits=[(S["pe"], pe_rb)], inc=S["dve"])
            d_rbs = S["dve"].n
            P.op("vector", lambda E: E.tensor_tensor(
                obuf2[:, :], obuf[:, :], rbsb[:, :], ALU.mult),
                waits=[(S["dve"], d_rbs), (S["act"], a_ob),
                       (S["ost"], S["ost"].n)],
                inc=S["dve"])
            o_done = S["dve"].n
            dma("sync", d_oh[h * c.DH:(h + 1) * c.DH, qt0:qt0 + c.QC],
                obuf2[:, :], S["ost"], waits=[(S["dve"], o_done)])
        th_rd = o_done
        setlast[hs] = pe_o
        vt_rd = pe_o
    barrier()

    dma("gpsimd", woW1[:, :], d_wo.rearrange("(kt p) d -> p kt d", p=128),
        S["w1a"])
    wo_ok = S["w1a"].n
    load_ln_tables(want_gate=True, want_battn=True)
    x_rd = 0
    o_rd = 0
    ln_prev = 0
    xch_act = 0
    st_prev = 0
    stats_fr = 0
    pb6_fr = 0
    pbfree5 = [0, 0]
    for ci in range(c.NC2):
        t0 = ci * c.TC2
        csl = slice(t0, t0 + c.TC2)
        ldsem = S["xa"] if ci % 2 == 0 else S["xb"]
        dma("sync", kt3(xchW1, c.TC2)[:, :, :], kt_slice(d_x1, csl), ldsem,
            waits=[(S["dve"], x_rd), (S["act"], xch_act),
                   (S["st"], st_prev)])
        xv = ldsem.n
        dma("sync", kt3(ochW2, c.TC2)[:, :, :], kt_slice(d_oh, csl), S["oc"],
            waits=[(S["pe"], o_rd)])
        ov = S["oc"].n
        dvx2 = 0
        for m in range(c.KT):
            b = m % 2
            for k in range(c.KT):
                w_ = ([(S["oc"], ov), (S["w1a"], wo_ok),
                       (S["dve"], pbfree5[b])] if k == 0 else ())
                P.op("tensor", (lambda k=k, m=m, b=b: (lambda E: E.matmul(
                    pb[b][:, 0:c.TC2],
                    woW1[:, k * c.D + m * 128: k * c.D + m * 128 + 128],
                    ochW2[:, k * c.TC2:(k + 1) * c.TC2],
                    start=(k == 0), stop=(k == c.KT - 1))))(),
                    waits=w_, inc=(S["pe"] if k == c.KT - 1 else None))
            pe_m = S["pe"].n
            o_rd = pe_m
            P.op("vector", (lambda m=m, b=b: (lambda E: E.scalar_tensor_tensor(
                xchW1[:, m * c.TC2:(m + 1) * c.TC2], pb[b][:, 0:c.TC2],
                battn[:, 3 * c.KT + m: 3 * c.KT + m + 1],
                xchW1[:, m * c.TC2:(m + 1) * c.TC2], ALU.add, ALU.add)))(),
                waits=[(S["pe"], pe_m), (S[ldsem.name], xv)], inc=S["dve"])
            dvx2 = S["dve"].n
            pbfree5[b] = dvx2
        x_rd = dvx2
        dma("sync", kt_slice(d_x2, csl), kt3(xchW1, c.TC2)[:, :, :], S["st"],
            waits=[(S["dve"], dvx2)])
        st_prev = S["st"].n
        P.op("scalar", lambda E: E.activation(x1bW2[:, :], xchW1[:, :], AF.Copy),
             waits=[(S["dve"], dvx2), (S["dve"], ln_prev)], inc=S["act"])
        a_b = S["act"].n
        xch_act = a_b
        P.op("scalar", lambda E: E.activation(xsqW2[:, :], x1bW2[:, :],
                                              AF.Square),
             waits=[(S["act"], a_b)], inc=S["act"])
        a_q = S["act"].n
        P.wait("tensor", S["act"], a_q)
        ln_prev, stats_fr, pb6_fr = emit_ln(
            x1bW2, xsqW2, xln, c.T, t0, router=True,
            stats_free=stats_fr, pb6_free=pb6_fr)
    barrier()

    emit_ffn()
    barrier()

    x_rd = 0
    st_prev = 0
    for ci in range(c.NC2):
        t0 = ci * c.TC2
        csl = slice(t0, t0 + c.TC2)
        ldsem = S["xa"] if ci % 2 == 0 else S["xb"]
        dma("sync", kt3(xchP7, c.TC2)[:, :, :], kt_slice(d_x2, csl), ldsem,
            waits=[(S["dve"], x_rd), (S["st"], st_prev)])
        xv = ldsem.n
        P.op("scalar", (lambda t0=t0: (lambda E: E.activation(
            kt3(yfP7, c.TC2)[:, :, :], kt3(ybuf, c.T)[:, :, t0:t0 + c.TC2],
            AF.Identity, scale=0.5)))(),
            waits=[(S["dve"], x_rd)], inc=S["act"])
        a_yf = S["act"].n
        P.op("vector", lambda E: E.tensor_tensor(
            xchP7[:, :], yfP7[:, :], xchP7[:, :], ALU.add),
            waits=[(S["act"], a_yf), (S[ldsem.name], xv)], inc=S["dve"])
        d_o = S["dve"].n
        x_rd = d_o
        dma("sync", kt_slice(d_out, csl), kt3(xchP7, c.TC2)[:, :, :], S["st"],
            waits=[(S["dve"], d_o)])
        st_prev = S["st"].n
    P.wait("sync", S["st"], S["st"].n)

    with nc.Block() as block:
        block.sync(lambda E: P.emit("sync", E))
        block.gpsimd(lambda E: P.emit("gpsimd", E))
        block.tensor(lambda E: P.emit("tensor", E))
        block.vector(lambda E: P.emit("vector", E))
        block.scalar(lambda E: P.emit("scalar", E))

    es.close()
    return nc, list(DI.keys())


def host_constants(c: Cfg):
    slopes = c.slopes
    p = np.arange(128, dtype=np.float32)[:, None]
    qf = np.arange(c.QC, dtype=np.float32)[None, :]
    alibi = np.stack([np.float32(s) * (p - qf) for s in slopes])
    alibid = np.zeros((c.H, 4, 128, c.QC), np.float32)
    for hi, s in enumerate(slopes):
        for v in range(4):
            o = v * 128
            base = np.float32(s) * (p - qf)
            alibid[hi, v] = np.where(qf < p + o, np.float32(-1e30), base)
    expc = np.zeros((128, c.H * 32), np.float32)
    for hi, s in enumerate(slopes):
        for oi in range(32):
            off = oi * 128 - 3584
            expc[:, hi * 32 + oi] = np.float32(s * off - c.CC)
    return alibi, alibid, expc


def make_inputs(c: Cfg, full):
    f32 = np.float32
    asc = np.ascontiguousarray
    alibi, alibid, expc = host_constants(c)
    w = {}
    w["wqT"] = asc(np.asarray(full["L0_attn_Wq"]).T.astype(f32))
    w["wkT"] = asc(np.asarray(full["L0_attn_Wk"]).T.astype(f32))
    w["wvT"] = asc(np.asarray(full["L0_attn_Wv"]).T.astype(f32))
    w["woT"] = asc(np.asarray(full["L0_attn_Wo"]).T.astype(f32))
    batt = np.stack([np.asarray(full["L0_attn_bq"]), np.asarray(full["L0_attn_bk"]),
                     np.asarray(full["L0_attn_bv"]), np.asarray(full["L0_attn_bo"])])
    w["battn"] = asc(batt.reshape(4, c.KT, 128).transpose(2, 0, 1)
                     .reshape(128, 4 * c.KT).astype(f32))
    w["gateT"] = asc(np.asarray(full["L0_ffn_gate_w"]).T.astype(f32))
    w["fc1wT"] = asc(np.asarray(full["L0_ffn_fc1_w"]).transpose(0, 2, 1).astype(f32))
    w["fc1b_sb"] = asc(np.asarray(full["L0_ffn_fc1_b"]).T
                       .reshape(2 * c.F // 128, 128, c.NE)
                       .transpose(1, 0, 2).reshape(128, -1).astype(f32))
    w["mult_sb"] = asc(np.asarray(full["L0_ffn_geglu_mult"]).T
                       .reshape(c.F // 128, 128, c.NE)
                       .transpose(1, 0, 2).reshape(128, -1).astype(f32))
    w["fc2wT"] = asc(np.asarray(full["L0_ffn_fc2_w"]).transpose(0, 2, 1).astype(f32))
    w["b2cat"] = asc(np.concatenate(
        [np.asarray(full["L0_ffn_fc2_b"]),
         np.asarray(full["L0_ffn_s_fc2_b"])[None]], 0).astype(f32))
    w["sfc1wT"] = asc(np.asarray(full["L0_ffn_s_fc1_w"]).T.astype(f32))
    w["sfc1b_sb"] = asc(np.asarray(full["L0_ffn_s_fc1_b"])
                        .reshape(2 * c.FS // 128, 128).T.astype(f32))
    w["smult_sb"] = asc(np.asarray(full["L0_ffn_s_geglu_mult"])
                        .reshape(c.FS // 128, 128).T.astype(f32))
    w["sfc2wT"] = asc(np.asarray(full["L0_ffn_s_fc2_w"]).T.astype(f32))
    w["ones1"] = np.ones((1, 128), f32)
    w["onescol"] = np.ones((128, 1), f32)
    w["idn"] = np.eye(128, dtype=f32)
    oneh = np.zeros((c.NE + 1, c.NE * 128), f32)
    for u in range(c.NE):
        oneh[u, u * 128:(u + 1) * 128] = 1.0
    w["onehot"] = oneh
    w["onesT"] = np.ones((1, c.T), f32)
    w["cellsv"] = np.array([[1e-5, 0, 0, 0, 0, 0, 0, 0]], f32)
    w["alibi"] = alibi
    w["alibid"] = alibid
    w["expconst"] = expc
    x = np.asarray(full["L0_attn_x"], dtype=f32)
    maps = []
    for b in range(x.shape[0]):
        m = dict(w)
        m["xT"] = asc(x[b].T)
        maps.append(m)
    return maps


_CFG = Cfg(T=4096, D=1024, F=1024, FS=2048, H=16, NE=15, TC=512, TC2=256)


def kernel(**inputs):
    cfg = _CFG
    maps = make_inputs(cfg, inputs)
    nc, _ = build_program(cfg)
    res = run_bass_kernel_spmd(nc, maps, core_ids=list(range(len(maps))))
    outs = [np.asarray(r["outT"], dtype=np.float32).T for r in res.results]
    return np.stack(outs, axis=0)


# version 06 (final)
# speedup vs baseline: 1.0707x; 1.0707x over previous
import math
import numpy as np
from contextlib import ExitStack

import concourse.bass as bass
import concourse.mybir as mybir
from concourse.bass_utils import run_bass_kernel_spmd

F32 = mybir.dt.float32
BF16 = mybir.dt.bfloat16
AF = mybir.ActivationFunctionType
ALU = mybir.AluOpType
AX = mybir.AxisListType

ENGINES = ("sync", "gpsimd", "tensor", "vector", "scalar")


class Cfg:
    def __init__(self, T=4096, D=1024, F=1024, FS=2048, H=16, NE=15,
                 TC=512, TC2=256):
        self.T, self.D, self.F, self.FS, self.H, self.NE = T, D, F, FS, H, NE
        self.TC, self.TC2 = TC, TC2
        self.KT = D // 128
        self.DH = 64
        self.NC = T // TC
        self.NC2 = T // TC2
        self.PB = F // 128
        self.SQ = FS // F
        self.NU = NE + self.SQ
        self.QC = TC
        self.NQC = T // self.QC
        self.KTT = T // 128
        self.CC = 16.0

    @property
    def slopes(self):
        n = self.H

        def pow2(n):
            start = 2.0 ** (-(2.0 ** (-(math.log2(n) - 3))))
            return [start * start ** i for i in range(n)]
        if math.log2(n).is_integer():
            return pow2(n)
        closest = 2 ** math.floor(math.log2(n))
        extra = pow2(2 * closest)[closest:]
        return pow2(closest) + extra[: n - closest]


class Sem:
    __slots__ = ("h", "n", "name")

    def __init__(self, es, nc, name):
        self.h = es.enter_context(nc.semaphore(name))
        self.n = 0
        self.name = name

    def bump(self, k=1):
        self.n += k
        return self.n


class Plan:
    def __init__(self):
        self.ops = {e: [] for e in ENGINES}

    def op(self, eng, fn, waits=(), inc=None, inck=1):
        if inc is not None:
            inc.bump(inck)
        self.ops[eng].append((fn, tuple(waits), inc, inck))

    def wait(self, eng, sem, val):
        self.ops[eng].append((None, ((sem, val),), None, 0))

    def emit(self, eng_name, eng):
        for fn, waits, inc, inck in self.ops[eng_name]:
            for (s, v) in waits:
                if v > 0:
                    eng.wait_ge(s.h, v)
            if fn is None:
                continue
            inst = fn(eng)
            if inc is not None:
                inst.then_inc(inc.h, inck)


def build_program(c: Cfg, dbg=False, stop_after=99):
    nc = bass.Bass("TRN2", target_bir_lowering=False)
    KIND = "ExternalOutput" if dbg else "Internal"
    es = ExitStack()
    DI = {}

    def din(name, shape, dtype=F32):
        DI[name] = nc.dram_tensor(name, list(shape), dtype,
                                  kind="ExternalInput").ap()
        return DI[name]

    d_xT = din("xT", (c.D, c.T))
    d_wq = din("wqT", (c.D, c.D))
    d_wk = din("wkT", (c.D, c.D))
    d_wv = din("wvT", (c.D, c.D))
    d_wo = din("woT", (c.D, c.D))
    d_battn = din("battn", (128, 4 * c.KT))
    d_gate = din("gateT", (c.D, c.NE))
    d_fc1w = din("fc1wT", (c.NE, c.D, 2 * c.F))
    d_fc1b = din("fc1b_sb", (128, (2 * c.F // 128) * c.NE))
    d_mult = din("mult_sb", (128, (c.F // 128) * c.NE))
    d_fc2w = din("fc2wT", (c.NE, c.F, c.D))
    d_b2cat = din("b2cat", (c.NE + 1, c.D))
    d_s1w = din("sfc1wT", (c.D, 2 * c.FS))
    d_s1b = din("sfc1b_sb", (128, 2 * c.FS // 128))
    d_smult = din("smult_sb", (128, c.FS // 128))
    d_s2w = din("sfc2wT", (c.FS, c.D))
    d_ones1 = din("ones1", (1, 128))
    d_onesc = din("onescol", (128, 1))
    d_idn = din("idn", (128, 128))
    d_oneh = din("onehot", (c.NE + 1, c.NE * 128))
    d_onesT = din("onesT", (1, c.T))
    d_cells = din("cellsv", (1, 8))
    d_alibi = din("alibi", (c.H, 128, 2 * c.QC))
    d_alibid = din("alibid", (c.H, 2, 128, 2 * c.QC))
    d_expc = din("expconst", (128, c.H * 32))

    d_out = nc.dram_tensor("outT", [c.D, c.T], F32, kind="ExternalOutput").ap()
    d_x1 = nc.dram_tensor("x1T", [c.D, c.T], F32, kind=KIND).ap()
    d_x2 = nc.dram_tensor("x2T", [c.D, c.T], F32, kind=KIND).ap()
    d_qh = nc.dram_tensor("qTh", [c.D, c.T], BF16, kind=KIND).ap()
    d_kh = nc.dram_tensor("kTh", [c.D, c.T], BF16, kind=KIND).ap()
    d_vh = nc.dram_tensor("vTh", [c.D, c.T], BF16, kind=KIND).ap()
    d_oh = nc.dram_tensor("oTh", [c.D, c.T], BF16, kind=KIND).ap()
    if dbg:
        d_dbg1 = nc.dram_tensor("dbg1", [128, c.KT * c.T], BF16,
                                kind="ExternalOutput").ap()
        d_dbg2 = nc.dram_tensor("dbg2", [128, c.KT * c.T], BF16,
                                kind="ExternalOutput").ap()
        d_dbg0 = nc.dram_tensor("dbg0", [128, c.KT * c.T], BF16,
                                kind="ExternalOutput").ap()

    SB_BASE = ((229376 - nc.sbuf_bytes_remaining + 95) // 32) * 32
    OFF = [SB_BASE]

    def sb(name, shape, dtype, at=None):
        esz = 2 if dtype == BF16 else 4
        nbytes = int(np.prod(shape[1:])) * esz
        if at is None:
            at = OFF[0]
            OFF[0] = at + ((nbytes + 31) // 32) * 32
        else:
            assert at + nbytes <= 229376, f"{name} overflows sbuf"
        return nc.alloc_sbuf_tensor_at(name, list(shape), dtype, offset=at)

    A0 = OFF[0]
    xln = sb("xln", (128, c.KT * c.T), BF16)
    _p3b = 2 * (c.KT * c.TC2 * 4) + 4 * (c.KT * c.TC2 * 2) + c.KT * c.D * 2 + 64
    OFF[0] = max(OFF[0], ((A0 + _p3b + 31) // 32) * 32)
    B0 = OFF[0]
    ybuf = sb("ybuf", (128, c.KT * c.T), BF16)
    _p4b = 6 * (2 * c.T) + ((c.KTT * (c.DH + 1) * 2 + 31) // 32) * 32 \
        + 3 * (2 * c.QC) + 4 * c.QC + 2 * c.QC + 4 * c.H * 32 + 4 * c.QC + 96
    OFF[0] = max(OFF[0], ((B0 + _p4b + 31) // 32) * 32)
    W1_0 = OFF[0]
    w1 = [sb(f"w1_{i}", (128, c.KT * 1024), BF16) for i in range(2)]
    W2_0 = OFF[0]
    HD = c.D // 2
    w2 = [sb(f"w2_{i}", (128, c.PB * HD), BF16) for i in range(2)]

    C0 = OFF[0]
    a_sb = sb("a_sb", (128, c.PB * c.TC), BF16)
    tmp_g = [sb(f"tmp_g{i}", (128, c.TC), F32) for i in range(2)]
    comb_sb = sb("comb_sb", (128, c.TC), F32)
    tmp_y = [sb(f"tmp_y{i}", (128, c.TC), BF16,
                at=C0 + c.PB * c.TC * 2 + i * c.TC * 4) for i in range(2)]
    C_EXT = C0 + c.PB * c.TC * 2 + 3 * c.TC * 4
    LNR0 = OFF[0]
    _lx = C0
    def _ln_sb(name, shape, dtype):
        nonlocal_holder[0] += 0
        t = sb(name, shape, dtype, at=_LX[0])
        esz = 2 if dtype == BF16 else 4
        _LX[0] += ((int(np.prod(shape[1:])) * esz + 31) // 32) * 32
        return t
    nonlocal_holder = [0]
    _LX = [C0]
    srow = _ln_sb("srow", (1, c.TC2), F32)
    qrow = _ln_sb("qrow", (1, c.TC2), F32)
    mrow = _ln_sb("mrow", (1, c.TC2), F32)
    m2row = _ln_sb("m2row", (1, c.TC2), F32)
    vrow = _ln_sb("vrow", (1, c.TC2), F32)
    rrow = _ln_sb("rrow", (1, c.TC2), F32)
    mbf = _ln_sb("mbf", (1, c.TC2), BF16)
    rbf = _ln_sb("rbf", (1, c.TC2), BF16)
    mbt = _ln_sb("mbt", (128, c.TC2), BF16)
    rbt = _ln_sb("rbt", (128, c.TC2), BF16)
    ubt = _ln_sb("ubt", (128, c.TC2), BF16)
    lg = _ln_sb("lg", (128, 16), F32)
    lgm = [_ln_sb(f"lgm{i}", (128, 16), F32) for i in range(3)]
    mx = [_ln_sb(f"mx{i}", (128, 1), F32) for i in range(4)]
    selt = _ln_sb("selt", (128, 16), F32)
    esel = _ln_sb("esel", (128, 16), F32)
    r3 = _ln_sb("r3", (128, 1), F32)
    combt = _ln_sb("combt", (128, 16), BF16)
    gate_b = _ln_sb("gate_b", (128, c.KT * c.NE), BF16)
    battn = _ln_sb("battn_s", (128, 4 * c.KT), F32)
    bk8 = _ln_sb("bk8", (128, c.KT), F32)
    cells = _ln_sb("cells", (1, 8), F32)
    assert _LX[0] <= C0 + 14336, f"LN set overflow {_LX[0] - C0}"
    th = sb("th", (128, 2 * c.QC), F32, at=C0)
    dgv = sb("dgv", (128, 4 * c.QC), F32, at=C0 + 8 * c.QC)
    sbt = [sb(f"sbt{i}", (128, 2 * c.QC), F32,
              at=C0 + 24 * c.QC + i * 8 * c.QC) for i in range(2)]
    _ax = C_EXT
    fc1b = sb("fc1b", (128, (2 * c.F // 128) * c.NE), F32, at=_ax)
    _ax += (2 * c.F // 128) * c.NE * 4
    mult = sb("mult", (128, (c.F // 128) * c.NE), F32, at=_ax)
    _ax += (c.F // 128) * c.NE * 4
    s1b = sb("s1b", (128, 2 * c.FS // 128), F32, at=_ax)
    _ax += (2 * c.FS // 128) * 4
    smult = sb("smult", (128, c.FS // 128), F32, at=_ax)
    _ax += (c.FS // 128) * 4
    b2c = sb("b2c", (c.NE + 1, c.D), BF16, at=_ax)
    _ax += 2 * c.D
    oneh = sb("oneh_s", (c.NE + 1, c.NE * 128), BF16, at=_ax)
    _ax += 2 * c.NE * 128
    OFF[0] = max(OFF[0], ((_ax + 31) // 32) * 32,
                 ((C0 + 40 * c.QC + 31) // 32) * 32)
    combT = sb("combT", (c.NE + 1, c.T), BF16)
    ones1 = sb("ones1_s", (1, 128), BF16)
    onesc = sb("onesc_s", (128, 1), BF16)
    idn = sb("idn_s", (128, 128), BF16)
    assert OFF[0] <= 229376, f"sbuf static overflow {OFF[0]}"

    tc2b4 = c.KT * c.TC2 * 4
    tc2b2 = c.KT * c.TC2 * 2

    def over(name, shape, dtype, at):
        esz = 2 if dtype == BF16 else 4
        assert at + int(np.prod(shape[1:])) * esz <= 229376, f"{name} overflow"
        return nc.alloc_sbuf_tensor_at(name, list(shape), dtype, offset=at)

    kt2 = (128, c.KT * c.TC2)
    xchA = over("xchA", kt2, F32, A0)
    yfA = over("yfA", kt2, F32, A0 + tc2b4)
    x1bA = over("x1bA", kt2, BF16, A0 + 2 * tc2b4)
    xsqA = over("xsqA", kt2, BF16, A0 + 2 * tc2b4 + tc2b2)
    xln2A = over("xln2A", kt2, BF16, A0 + 2 * tc2b4 + 2 * tc2b2)
    qstA = over("qstA", kt2, BF16, A0 + 2 * tc2b4 + 3 * tc2b2)
    wvA = over("wvA", (128, c.KT * c.D), BF16, A0 + 2 * tc2b4 + 4 * tc2b2)
    kstW = over("kstW", kt2, BF16, W2_0)
    vstW = over("vstW", kt2, BF16, W2_0 + tc2b2)
    xbW = over("xbW", kt2, BF16, W2_0)
    xqW = over("xqW", kt2, BF16, W2_0 + tc2b2)
    woW1 = over("woW1", (128, c.KT * c.D), BF16, W1_0)
    xchW1 = over("xchW1", kt2, F32, W1_0 + c.KT * c.D * 2)
    x1bW2 = over("x1bW2", kt2, BF16, W2_0)
    xsqW2 = over("xsqW2", kt2, BF16, W2_0 + tc2b2)
    ochW2 = over("ochW2", kt2, BF16, W2_0 + 2 * tc2b2)
    assert 3 * tc2b2 <= c.PB * HD * 2 * 2, "W2 overlay overflow"
    xchP7 = over("xchP7", kt2, F32, W1_0)
    yfP7 = over("yfP7", kt2, F32, W1_0 + tc2b4)
    hb = 2 * c.T
    qhB = [over(f"qhB{i}", (c.DH, c.T), BF16, B0 + i * hb) for i in range(2)]
    khB = [over(f"khB{i}", (c.DH, c.T), BF16, B0 + (2 + i) * hb) for i in range(2)]
    vhB = [over(f"vhB{i}", (c.DH, c.T), BF16, B0 + (4 + i) * hb) for i in range(2)]
    _b = B0 + 6 * hb
    vtB = over("vtB", (128, c.KTT * (c.DH + 1)), BF16, _b)
    _b += ((c.KTT * (c.DH + 1) * 2 + 31) // 32) * 32
    obuf = over("obuf", (c.DH, c.QC), BF16, _b); _b += 2 * c.QC
    obuf2 = over("obuf2", (c.DH, c.QC), BF16, _b); _b += 2 * c.QC
    rbsb = over("rbsb", (c.DH, c.QC), BF16, _b); _b += 2 * c.QC
    rrw = over("rrw", (1, c.QC), F32, _b); _b += 4 * c.QC
    rrb = over("rrb", (1, c.QC), BF16, _b); _b += 2 * c.QC
    expc = over("expc", (128, c.H * 32), F32, _b); _b += 4 * c.H * 32
    ptt = [over(f"ptt{i}", (128, 2 * c.QC), BF16, _b + i * 4 * c.QC)
           for i in range(2)]
    _b += 8 * c.QC
    assert _b <= OFF[0], "arena B overflow"

    pbA = es.enter_context(nc.psum_tensor("pbA", [128, 1024], F32))
    pbB = es.enter_context(nc.psum_tensor("pbB", [128, 1024], F32))
    pb = [pbA[:, 0:512], pbA[:, 512:1024], pbB[:, 0:512], pbB[:, 512:1024]]
    pb.append(es.enter_context(nc.psum_tensor("pb4", [128, 512], F32))[:, :])
    pb.append(es.enter_context(nc.psum_tensor("pb5", [128, 512], F32))[:, :])
    pb.append(es.enter_context(nc.psum_tensor("pb6", [128, 512], F32))[:, :])
    pbt = es.enter_context(nc.psum_tensor("pbt", [128, 1024], BF16))

    S = {}
    for name in ("ld", "ldg", "st", "ost", "w1a", "w1b", "w2a", "w2b",
                 "xa", "xb", "xga", "xgb", "oc", "h0", "h1", "qst", "kst",
                 "vst", "thld",
                 "pe", "dve", "act", "gpc",
                 "b_sync", "b_gpsimd", "b_tensor", "b_vector", "b_scalar"):
        S[name] = Sem(es, nc, name)
    P = Plan()
    barno = [0]

    def barrier():
        barno[0] += 1
        n = barno[0]
        for e in ENGINES:
            if e == "sync":
                P.wait(e, S["st"], S["st"].n)
                P.wait(e, S["ost"], S["ost"].n)
                P.wait(e, S["ld"], S["ld"].n)
                for nm in ("qst", "kst", "vst", "xa", "xb", "oc",
                           "h0", "h1"):
                    P.wait(e, S[nm], S[nm].n)
            if e == "gpsimd":
                for nm in ("ldg", "w1a", "w1b", "w2a", "w2b", "thld",
                           "xga", "xgb"):
                    P.wait(e, S[nm], S[nm].n)
            P.op(e, lambda E: E.drain(), inc=S["b_" + e])
        for e in ENGINES:
            for o in ENGINES:
                if o != e:
                    P.wait(e, S["b_" + o], n)

    def dma(eng, out, in_, sem, waits=()):
        P.op(eng, (lambda o, i: (lambda E: E.dma_start(out=o, in_=i)))(out, in_),
             waits=waits, inc=sem, inck=16)

    def kt_slice(d_t, csl):
        return d_t.rearrange("(kt p) t -> p kt t", p=128)[:, :, csl]

    def kt3(sbuf_t, n_t):
        return sbuf_t.rearrange("p (kt t) -> p kt t", t=n_t)

    dma("gpsimd", combT[c.NE:c.NE + 1, :], d_onesT, S["ldg"])
    dma("gpsimd", ones1[:, :], d_ones1, S["ldg"])
    dma("gpsimd", onesc[:, :], d_onesc, S["ldg"])
    dma("gpsimd", idn[:, :], d_idn, S["ldg"])

    def load_ln_tables(want_gate, want_battn):
        dma("gpsimd", cells[:, :], d_cells, S["ldg"])
        if want_gate:
            dma("gpsimd", gate_b[:, :],
                d_gate.rearrange("(kt p) e -> p kt e", p=128), S["ldg"])
        if want_battn:
            dma("gpsimd", battn[:, :], d_battn, S["ldg"])
        n = S["ldg"].n
        for e in ("scalar", "vector", "tensor"):
            P.wait(e, S["ldg"], n)
        if want_battn:
            P.op("vector", lambda E: E.tensor_scalar(
                bk8[:, :], battn[:, c.KT:2 * c.KT], 0.125, None, ALU.mult),
                inc=S["dve"])
            return S["dve"].n
        return 0
    barrier()

    def emit_ln(src_bf, src_sq, dst, dst_stride, dst_t0, router,
                pe_ready=0, dve_war=0, stats_free=0, pb6_free=0):
        for kt in range(c.KT):
            P.op("tensor", (lambda k=kt: (lambda E: E.matmul(
                pb[4][0:1, 0:c.TC2], onesc[:, :],
                src_bf[:, k * c.TC2:(k + 1) * c.TC2],
                start=(k == 0), stop=(k == c.KT - 1))))(),
                waits=([(S["pe"], pe_ready), (S["dve"], stats_free)]
                       if kt == 0 else ()),
                inc=(S["pe"] if kt == c.KT - 1 else None))
        pe_sum = S["pe"].n
        for kt in range(c.KT):
            P.op("tensor", (lambda k=kt: (lambda E: E.matmul(
                pb[5][0:1, 0:c.TC2], onesc[:, :],
                src_sq[:, k * c.TC2:(k + 1) * c.TC2],
                start=(k == 0), stop=(k == c.KT - 1))))(),
                waits=([(S["dve"], stats_free)] if kt == 0 else ()),
                inc=(S["pe"] if kt == c.KT - 1 else None))
        pe_sq = S["pe"].n
        P.op("vector", lambda E: E.tensor_scalar(
            mrow[0:1, :], pb[4][0:1, 0:c.TC2], 1.0 / c.D, None, ALU.mult),
            waits=[(S["pe"], pe_sum)], inc=S["dve"])
        d_m = S["dve"].n
        P.op("vector", lambda E: E.tensor_scalar(
            qrow[0:1, :], pb[5][0:1, 0:c.TC2], 1.0 / c.D, None, ALU.mult),
            waits=[(S["pe"], pe_sq)], inc=S["dve"])
        stats_rd = S["dve"].n
        P.op("vector", lambda E: E.tensor_tensor(
            m2row[0:1, :], mrow[0:1, :], mrow[0:1, :], ALU.mult),
            waits=[(S["dve"], S["dve"].n)], inc=S["dve"])
        P.op("vector", lambda E: E.tensor_tensor(
            vrow[0:1, :], qrow[0:1, :], m2row[0:1, :], ALU.subtract),
            waits=[(S["dve"], S["dve"].n)], inc=S["dve"])
        d_v = S["dve"].n
        P.op("scalar", lambda E: E.activation(
            vrow[0:1, :], vrow[0:1, :], AF.Ln, bias=cells[0:1, 0:1]),
            waits=[(S["dve"], d_v)], inc=S["act"])
        P.op("scalar", lambda E: E.activation(
            rrow[0:1, :], vrow[0:1, :], AF.Exp, scale=-0.5),
            waits=[(S["act"], S["act"].n)], inc=S["act"])
        a_r = S["act"].n
        P.op("vector", lambda E: E.tensor_copy(mbf[0:1, :], mrow[0:1, :]),
             waits=[(S["dve"], d_m)], inc=S["dve"])
        d_mb = S["dve"].n
        P.op("vector", lambda E: E.tensor_copy(rbf[0:1, :], rrow[0:1, :]),
             waits=[(S["act"], a_r)], inc=S["dve"])
        d_rb = S["dve"].n
        P.op("tensor", lambda E: E.matmul(
            pb[2][:, 0:c.TC2], ones1[:, :], mbf[0:1, :], start=True, stop=True),
            waits=[(S["dve"], d_mb)], inc=S["pe"])
        pe_mb = S["pe"].n
        P.op("tensor", lambda E: E.matmul(
            pb[3][:, 0:c.TC2], ones1[:, :], rbf[0:1, :], start=True, stop=True),
            waits=[(S["dve"], d_rb)], inc=S["pe"])
        pe_rb = S["pe"].n
        P.op("vector", lambda E: E.tensor_copy(mbt[:, :], pb[2][:, 0:c.TC2]),
             waits=[(S["pe"], pe_mb)], inc=S["dve"])
        P.op("vector", lambda E: E.tensor_copy(rbt[:, :], pb[3][:, 0:c.TC2]),
             waits=[(S["pe"], pe_rb)], inc=S["dve"])
        d_rbt = S["dve"].n
        for kt in range(c.KT):
            P.op("vector", (lambda k=kt: (lambda E: E.tensor_tensor(
                ubt[:, :], src_bf[:, k * c.TC2:(k + 1) * c.TC2], mbt[:, :],
                ALU.subtract)))(),
                waits=[(S["dve"], max(d_rbt, S["dve"].n))],
                inc=S["dve"])
            d_u = S["dve"].n
            P.op("vector", (lambda k=kt: (lambda E: E.tensor_tensor(
                dst[:, k * dst_stride + dst_t0: k * dst_stride + dst_t0 + c.TC2],
                ubt[:, :], rbt[:, :], ALU.mult)))(),
                waits=[(S["dve"], d_u)] +
                      ([(S["pe"], dve_war)] if (dve_war and kt == 0) else []),
                inc=S["dve"])
        dve_ln = S["dve"].n
        if not router:
            return dve_ln, stats_rd, pb6_free
        tp_free = 0
        for tt in range(c.TC2 // 128):
            tg = dst_t0 + tt * 128
            for kt in range(c.KT):
                P.op("tensor", (lambda k=kt, tg=tg: (lambda E: E.matmul(
                    pb[6][:, 0:c.NE],
                    dst[:, k * dst_stride + tg: k * dst_stride + tg + 128],
                    gate_b[:, k * c.NE:(k + 1) * c.NE],
                    start=(k == 0), stop=(k == c.KT - 1))))(),
                    waits=([(S["dve"], dve_ln), (S["dve"], pb6_free)]
                           if kt == 0 else ()),
                    inc=(S["pe"] if kt == c.KT - 1 else None))
            pe_lg = S["pe"].n
            P.op("vector", lambda E: E.tensor_copy(lg[:, 0:c.NE],
                                                   pb[6][:, 0:c.NE]),
                 waits=[(S["pe"], pe_lg)], inc=S["dve"])
            pb6_free = S["dve"].n
            P.op("vector", lambda E: E.reduce_max(mx[0][:, :], lg[:, 0:c.NE],
                                                  axis=AX.X),
                 waits=[(S["dve"], S["dve"].n)], inc=S["dve"])
            P.op("vector", lambda E: E.tensor_scalar(
                lgm[1][:, 0:c.NE], lg[:, 0:c.NE], mx[0][:, 0:1], None, ALU.is_ge),
                waits=[(S["dve"], S["dve"].n)], inc=S["dve"])
            P.op("vector", lambda E: E.scalar_tensor_tensor(
                lgm[0][:, 0:c.NE], lgm[1][:, 0:c.NE], -1e30, lg[:, 0:c.NE],
                ALU.mult, ALU.add),
                waits=[(S["dve"], S["dve"].n)], inc=S["dve"])
            P.op("vector", lambda E: E.reduce_max(mx[1][:, :], lgm[0][:, 0:c.NE],
                                                  axis=AX.X),
                 waits=[(S["dve"], S["dve"].n)], inc=S["dve"])
            P.op("vector", lambda E: E.tensor_scalar(
                lgm[2][:, 0:c.NE], lgm[0][:, 0:c.NE], mx[1][:, 0:1], None,
                ALU.is_ge),
                waits=[(S["dve"], S["dve"].n)], inc=S["dve"])
            P.op("vector", lambda E: E.scalar_tensor_tensor(
                lgm[0][:, 0:c.NE], lgm[2][:, 0:c.NE], -1e30, lgm[0][:, 0:c.NE],
                ALU.mult, ALU.add),
                waits=[(S["dve"], S["dve"].n)], inc=S["dve"])
            P.op("vector", lambda E: E.reduce_max(mx[1][:, :], lgm[0][:, 0:c.NE],
                                                  axis=AX.X),
                 waits=[(S["dve"], S["dve"].n)], inc=S["dve"])
            d_m3 = S["dve"].n
            P.op("vector", lambda E: E.tensor_scalar(
                selt[:, 0:c.NE], lg[:, 0:c.NE], mx[1][:, 0:1], None, ALU.is_ge),
                waits=[(S["dve"], d_m3)], inc=S["dve"])
            d_sel = S["dve"].n
            P.op("vector", lambda E: E.tensor_scalar(
                mx[2][:, :], mx[0][:, :], -1.0, None, ALU.mult),
                waits=[(S["dve"], d_m3)], inc=S["dve"])
            d_m1n = S["dve"].n
            P.op("scalar", lambda E: E.activation(
                lgm[1][:, 0:c.NE], lg[:, 0:c.NE], AF.Exp, bias=mx[2][:, 0:1]),
                waits=[(S["dve"], d_m1n)], inc=S["act"])
            a_e = S["act"].n
            P.op("vector", lambda E: E.tensor_tensor(
                esel[:, 0:c.NE], lgm[1][:, 0:c.NE], selt[:, 0:c.NE], ALU.mult),
                waits=[(S["act"], a_e), (S["dve"], d_sel)], inc=S["dve"])
            P.op("vector", lambda E: E.reduce_sum(mx[3][:, :], esel[:, 0:c.NE],
                                                  axis=AX.X),
                 waits=[(S["dve"], S["dve"].n)], inc=S["dve"])
            P.op("vector", lambda E: E.reciprocal(r3[:, :], mx[3][:, :]),
                 waits=[(S["dve"], S["dve"].n)], inc=S["dve"])
            P.op("vector", lambda E: E.tensor_scalar(
                combt[:, 0:c.NE], esel[:, 0:c.NE], r3[:, 0:1], None, ALU.mult),
                waits=[(S["dve"], S["dve"].n)], inc=S["dve"])
            d_cb = S["dve"].n
            P.op("tensor", lambda E: E.transpose(
                pbt[0:c.NE, 0:128], combt[:, 0:c.NE], idn[:, :]),
                waits=[(S["dve"], d_cb), (S["dve"], tp_free)], inc=S["pe"])
            pe_tp = S["pe"].n
            P.op("vector", (lambda tg=tg: (lambda E: E.tensor_copy(
                combT[0:c.NE, tg:tg + 128], pbt[0:c.NE, 0:128])))(),
                waits=[(S["pe"], pe_tp)], inc=S["dve"])
            tp_free = S["dve"].n
        return S["dve"].n, stats_rd, pb6_free

    load_ln_tables(want_gate=True, want_battn=False)
    prev_done = 0
    stats_fr = 0
    pb6_fr = 0
    for ci in range(c.NC2):
        t0 = ci * c.TC2
        csl = slice(t0, t0 + c.TC2)
        ldsem = S["xga"] if ci % 2 == 0 else S["xgb"]
        dma("gpsimd", kt3(xbW, c.TC2)[:, :, :], kt_slice(d_xT, csl), ldsem,
            waits=[(S["dve"], prev_done)])
        xv = ldsem.n
        P.op("scalar", lambda E: E.activation(xqW[:, :], xbW[:, :], AF.Square),
             waits=[(S[ldsem.name], xv), (S["dve"], prev_done)], inc=S["act"])
        a_sq = S["act"].n
        P.wait("tensor", S[ldsem.name], xv)
        P.wait("tensor", S["act"], a_sq)
        prev_done, stats_fr, pb6_fr = emit_ln(
            xbW, xqW, xln, c.T, t0, router=True,
            stats_free=stats_fr, pb6_free=pb6_fr)
    barrier()
    if dbg:
        dma("sync", d_dbg0, xln[:, :], S["st"])
        for _e in ENGINES:
            P.wait(_e, S["st"], S["st"].n)

    FT = c.PB

    def emit_ffn():
        def w1_src(u, h):
            if u < c.NE:
                base = d_fc1w[u].rearrange("(kt p) f -> p kt f", p=128)
                xo, go = 4 * h * 128, (c.PB + 4 * h) * 128
            else:
                sq = u - c.NE
                base = d_s1w.rearrange("(kt p) f -> p kt f", p=128)
                xo = (sq * c.PB + 4 * h) * 128
                go = ((c.FS // 128) + sq * c.PB + 4 * h) * 128
            return base[:, :, xo:xo + 512], base[:, :, go:go + 512]

        def w2_src(u, j):
            if u < c.NE:
                return d_fc2w[u].rearrange("(kt p) d -> p kt d", p=128)[
                    :, :, j * HD:(j + 1) * HD]
            sq = u - c.NE
            return d_s2w.rearrange("(kt p) d -> p kt d", p=128)[
                :, sq * c.PB:(sq + 1) * c.PB, j * HD:(j + 1) * HD]

        def bias_cols(u, i):
            if u < c.NE:
                return (fc1b[:, i * c.NE + u: i * c.NE + u + 1],
                        fc1b[:, (c.PB + i) * c.NE + u:(c.PB + i) * c.NE + u + 1])
            sq = u - c.NE
            return (s1b[:, sq * c.PB + i: sq * c.PB + i + 1],
                    s1b[:, c.FS // 128 + sq * c.PB + i:
                        c.FS // 128 + sq * c.PB + i + 1])

        def mult_col(u, kt):
            if u < c.NE:
                return mult[:, kt * c.NE + u: kt * c.NE + u + 1]
            sq = u - c.NE
            return smult[:, sq * c.PB + kt: sq * c.PB + kt + 1]

        dma("gpsimd", b2c[:, :], d_b2cat, S["ldg"])
        dma("gpsimd", oneh[:, :], d_oneh, S["ldg"])
        dma("gpsimd", fc1b[:, :], d_fc1b, S["ldg"])
        dma("gpsimd", mult[:, :], d_mult, S["ldg"])
        dma("gpsimd", s1b[:, :], d_s1b, S["ldg"])
        dma("gpsimd", smult[:, :], d_smult, S["ldg"])
        ldg_ok = S["ldg"].n
        P.wait("scalar", S["ldg"], ldg_ok)
        P.wait("vector", S["ldg"], ldg_ok)
        P.wait("tensor", S["ldg"], ldg_ok)
        w1sem = [S["w1a"], S["w1b"]]
        w2sem = [S["w2a"], S["w2b"]]
        w1_free = [S["pe"].n, S["pe"].n]
        w2_free = [S["pe"].n, S["pe"].n]
        w2_rdy = [0, 0]
        a_war = S["pe"].n
        comb_war = 0
        comb_bank_free = 0
        comb_rdy = 0
        pxh, pg, py = [pb[0], pb[1]], [pb[2], pb[3]], [pb[4], pb[5]]
        pxh_free = [0, 0]
        pg_free_act = [0, 0]
        py_free = [0, 0]
        tmpg_war = [0, 0]
        tmpy_rd = [0, 0]

        for u in range(c.NU):
            for h in range(2):
                xs, gs = w1_src(u, h)
                sem = w1sem[h]
                wdst = kt3(w1[h], 1024)
                dma("gpsimd", wdst[:, :, 0:512], xs, sem,
                    waits=[(S["pe"], w1_free[h])])
                dma("gpsimd", wdst[:, :, 512:1024], gs, sem)
            for j in range(2):
                sem = w2sem[j]
                dma("gpsimd", kt3(w2[j], HD)[:, :, :], w2_src(u, j), sem,
                    waits=[(S["pe"], w2_free[j])])
                wv2 = sem.n
                for kt in range(FT):
                    P.op("vector", (lambda j=j, kt=kt, u=u: (lambda E:
                         E.tensor_scalar(
                             w2[j][:, kt * HD:(kt + 1) * HD],
                             w2[j][:, kt * HD:(kt + 1) * HD],
                             mult_col(u, kt), None, ALU.mult)))(),
                        waits=([(S[sem.name], wv2)] if kt == 0 else
                               [(S["dve"], S["dve"].n)]),
                        inc=S["dve"])
                w2_rdy[j] = S["dve"].n
            w1_lastuse = [None, None]
            w2_lastuse = [None, None]
            for ch in range(c.NC):
                t0 = ch * c.TC
                if u < c.NE:
                    P.op("tensor", (lambda u=u, t0=t0: (lambda E: E.matmul(
                        pb[6][:, 0:c.TC], oneh[:, u * 128:(u + 1) * 128],
                        combT[0:c.NE + 1, t0:t0 + c.TC],
                        start=True, stop=True)))(),
                        waits=[(S["dve"], comb_bank_free),
                               (S["ldg"], ldg_ok)], inc=S["pe"])
                    pe_cb = S["pe"].n
                    P.op("vector", lambda E: E.tensor_copy(
                        comb_sb[:, :], pb[6][:, 0:c.TC]),
                        waits=[(S["pe"], pe_cb), (S["dve"], comb_war)],
                        inc=S["dve"])
                    comb_bank_free = S["dve"].n
                    comb_rdy = S["dve"].n
                for i in range(c.PB):
                    b = i % 2
                    h = i // 4
                    wsem = w1sem[h]
                    wv = wsem.n
                    wcol = (i % 4) * 128
                    for k in range(c.KT):
                        P.op("tensor", (lambda h=h, k=k, wcol=wcol, t0=t0, b=b:
                             (lambda E: E.matmul(
                                 pxh[b][:, 0:c.TC],
                                 w1[h][:, k * 1024 + wcol: k * 1024 + wcol + 128],
                                 xln[:, k * c.T + t0: k * c.T + t0 + c.TC],
                                 start=(k == 0), stop=(k == c.KT - 1))))(),
                            waits=([(S[wsem.name], wv),
                                    (S["dve"], pxh_free[b])] if k == 0 else ()))
                    for k in range(c.KT):
                        P.op("tensor", (lambda h=h, k=k, wcol=wcol, t0=t0, b=b:
                             (lambda E: E.matmul(
                                 pg[b][:, 0:c.TC],
                                 w1[h][:, k * 1024 + 512 + wcol:
                                       k * 1024 + 512 + wcol + 128],
                                 xln[:, k * c.T + t0: k * c.T + t0 + c.TC],
                                 start=(k == 0), stop=(k == c.KT - 1))))(),
                            waits=([(S["act"], pg_free_act[b])] if k == 0 else ()),
                            inc=(S["pe"] if k == c.KT - 1 else None))
                    pe_pair = S["pe"].n
                    if ch == c.NC - 1 and i % 4 == 3:
                        w1_lastuse[h] = pe_pair
                    bxh, bg = bias_cols(u, i)
                    P.op("scalar", (lambda b=b, bg=bg: (lambda E: E.activation(
                        tmp_g[b][:, :], pg[b][:, 0:c.TC], AF.Gelu, bias=bg)))(),
                        waits=[(S["pe"], pe_pair), (S["dve"], tmpg_war[b]),
                               (S["dve"], tmpy_rd[b])],
                        inc=S["act"])
                    pg_free_act[b] = S["act"].n
                    g_done = S["act"].n
                    P.op("vector", (lambda b=b, i=i, bxh=bxh: (lambda E:
                         E.scalar_tensor_tensor(
                             a_sb[:, i * c.TC:(i + 1) * c.TC],
                             pxh[b][:, 0:c.TC], bxh, tmp_g[b][:, :],
                             ALU.add, ALU.mult)))(),
                        waits=[(S["act"], g_done), (S["pe"], a_war)],
                        inc=S["dve"])
                    pxh_free[b] = S["dve"].n
                    tmpg_war[b] = S["dve"].n
                a_all = S["dve"].n
                for m in range(c.KT):
                    b = m % 2
                    j = m // (c.KT // 2)
                    wsem = w2sem[j]
                    wv = wsem.n
                    last_u = (u == c.NU - 1)
                    for k in range(FT):
                        P.op("tensor", (lambda j=j, k=k, m=m, b=b, last_u=last_u:
                             (lambda E: E.matmul(
                                 py[b][:, 0:c.TC],
                                 w2[j][:, k * HD + (m % (c.KT // 2)) * 128:
                                       k * HD + (m % (c.KT // 2)) * 128 + 128],
                                 a_sb[:, k * c.TC:(k + 1) * c.TC],
                                 start=(k == 0),
                                 stop=(k == FT - 1 and not last_u))))(),
                            waits=([(S[wsem.name], wv),
                                    (S["dve"], max(a_all, w2_rdy[j])),
                                    (S["dve"], py_free[b])] if k == 0 else ()),
                            inc=(S["pe"] if (k == FT - 1 and not last_u)
                                 else None))
                    if last_u:
                        P.op("tensor", (lambda m=m, b=b, t0=t0: (lambda E:
                             E.matmul(
                                 py[b][:, 0:c.TC],
                                 b2c[0:c.NE + 1, m * 128:(m + 1) * 128],
                                 combT[0:c.NE + 1, t0:t0 + c.TC],
                                 start=False, stop=True)))(),
                            inc=S["pe"])
                    pe_y = S["pe"].n
                    if ch == c.NC - 1 and (m % (c.KT // 2)) == c.KT // 2 - 1:
                        w2_lastuse[j] = pe_y
                    ysl = ybuf[:, m * c.T + t0: m * c.T + t0 + c.TC]
                    tbuf = tmp_y[m % 2]
                    if u == 0:
                        P.op("vector", (lambda ysl=ysl, b=b: (lambda E:
                             E.tensor_tensor(
                                 ysl, py[b][:, 0:c.TC], comb_sb[:, :],
                                 ALU.mult)))(),
                            waits=[(S["pe"], pe_y), (S["dve"], comb_rdy)],
                            inc=S["dve"])
                        py_free[b] = S["dve"].n
                        comb_war = S["dve"].n
                    else:
                        if u < c.NE:
                            P.op("vector", (lambda tbuf=tbuf, b=b: (lambda E:
                                 E.tensor_tensor(
                                     tbuf[:, :], py[b][:, 0:c.TC],
                                     comb_sb[:, :], ALU.mult)))(),
                                waits=[(S["pe"], pe_y), (S["dve"], comb_rdy),
                                       (S["dve"], tmpy_rd[m % 2])],
                                inc=S["dve"])
                            comb_war = S["dve"].n
                        else:
                            P.op("vector", (lambda tbuf=tbuf, b=b: (lambda E:
                                 E.tensor_copy(tbuf[:, :], py[b][:, 0:c.TC])))(),
                                waits=[(S["pe"], pe_y),
                                       (S["dve"], tmpy_rd[m % 2])],
                                inc=S["dve"])
                        py_free[b] = S["dve"].n
                        d_t = S["dve"].n
                        P.op("vector", (lambda ysl=ysl, tbuf=tbuf: (lambda E:
                             E.tensor_tensor(ysl, ysl, tbuf[:, :], ALU.add)))(),
                            waits=[(S["dve"], d_t)], inc=S["dve"])
                        tmpy_rd[m % 2] = S["dve"].n
                a_war = pe_y
            w1_free = [w1_lastuse[0], w1_lastuse[1]]
            w2_free = [w2_lastuse[0], w2_lastuse[1]]

    emit_ffn()
    barrier()
    if dbg:
        dma("sync", d_dbg1, xln[:, :], S["st"])
        dma("sync", d_dbg2, ybuf[:, :], S["st"])
        for _e in ENGINES:
            P.wait(_e, S["st"], S["st"].n)
    if stop_after <= 2:
        P.wait("sync", S["st"], S["st"].n)
        with nc.Block() as block:
            block.sync(lambda E: P.emit("sync", E))
            block.gpsimd(lambda E: P.emit("gpsimd", E))
            block.tensor(lambda E: P.emit("tensor", E))
            block.vector(lambda E: P.emit("vector", E))
            block.scalar(lambda E: P.emit("scalar", E))
        es.close()
        return nc, list(DI.keys())

    dma("gpsimd", w1[0][:, :], d_wq.rearrange("(kt p) d -> p kt d", p=128),
        S["w1a"])
    dma("gpsimd", w1[1][:, :], d_wk.rearrange("(kt p) d -> p kt d", p=128),
        S["w1b"])
    dma("gpsimd", wvA[:, :], d_wv.rearrange("(kt p) d -> p kt d", p=128),
        S["w2a"])
    wq_ok, wk_ok, wv_ok = S["w1a"].n, S["w1b"].n, S["w2a"].n
    bk8_ok = load_ln_tables(want_gate=False, want_battn=True)
    x_rd = 0
    qkv_prev = 0
    ln_prev = 0
    xch_act = 0
    st_prev = 0
    stats_fr = 0
    for ci in range(c.NC2):
        t0 = ci * c.TC2
        csl = slice(t0, t0 + c.TC2)
        ldsem = S["xa"] if ci % 2 == 0 else S["xb"]
        dma("sync", kt3(xchA, c.TC2)[:, :, :], kt_slice(d_xT, csl), ldsem,
            waits=[(S["dve"], x_rd), (S["act"], xch_act),
                   (S["st"], st_prev)])
        xv = ldsem.n
        P.op("scalar", (lambda t0=t0: (lambda E: E.activation(
            kt3(yfA, c.TC2)[:, :, :], kt3(ybuf, c.T)[:, :, t0:t0 + c.TC2],
            AF.Identity, scale=0.5)))(),
            waits=[(S["dve"], x_rd)], inc=S["act"])
        a_yf = S["act"].n
        P.op("vector", lambda E: E.tensor_tensor(
            xchA[:, :], yfA[:, :], xchA[:, :], ALU.add),
            waits=[(S["act"], a_yf), (S[ldsem.name], xv)], inc=S["dve"])
        dvx1 = S["dve"].n
        x_rd = dvx1
        dma("sync", kt_slice(d_x1, csl), kt3(xchA, c.TC2)[:, :, :], S["st"],
            waits=[(S["dve"], dvx1)])
        st_prev = S["st"].n
        P.op("scalar", lambda E: E.activation(x1bA[:, :], xchA[:, :], AF.Copy),
             waits=[(S["dve"], dvx1), (S["dve"], ln_prev)], inc=S["act"])
        a_b = S["act"].n
        xch_act = a_b
        P.op("scalar", lambda E: E.activation(xsqA[:, :], x1bA[:, :], AF.Square),
             waits=[(S["act"], a_b)], inc=S["act"])
        a_q = S["act"].n
        P.wait("tensor", S["act"], a_q)
        ln_prev, stats_fr, _ = emit_ln(x1bA, xsqA, xln2A, c.TC2, 0,
                                       router=False, dve_war=qkv_prev,
                                       stats_free=stats_fr)
        pbfree = [0, 0]
        for mat, wbuf, wok, stg, stsem, dmat in (
                ("q", w1[0], wq_ok, qstA, S["qst"], d_qh),
                ("k", w1[1], wk_ok, kstW, S["kst"], d_kh),
                ("v", wvA, wv_ok, vstW, S["vst"], d_vh)):
            bcol = {"q": 0, "k": 1, "v": 2}[mat]
            for m in range(c.KT):
                b = m % 2
                for k in range(c.KT):
                    w_ = ([(S["dve"], ln_prev),
                           ({"q": S["w1a"], "k": S["w1b"], "v": S["w2a"]}[mat],
                            wok),
                           (S["act"], pbfree[b])] if k == 0 else ())
                    P.op("tensor", (lambda wbuf=wbuf, k=k, m=m, b=b: (lambda E:
                         E.matmul(
                             pb[b][:, 0:c.TC2],
                             wbuf[:, k * c.D + m * 128: k * c.D + m * 128 + 128],
                             xln2A[:, k * c.TC2:(k + 1) * c.TC2],
                             start=(k == 0), stop=(k == c.KT - 1))))(),
                        waits=w_, inc=(S["pe"] if k == c.KT - 1 else None))
                pe_m = S["pe"].n
                qkv_prev = pe_m
                if mat == "k":
                    bias_ap, sc = bk8[:, m:m + 1], 0.125
                else:
                    bias_ap, sc = battn[:, bcol * c.KT + m:
                                        bcol * c.KT + m + 1], 1.0
                P.op("scalar", (lambda stg=stg, m=m, bias_ap=bias_ap, sc=sc,
                     b=b: (lambda E: E.activation(
                         stg[:, m * c.TC2:(m + 1) * c.TC2], pb[b][:, 0:c.TC2],
                         AF.Identity, bias=bias_ap, scale=sc)))(),
                    waits=[(S["pe"], pe_m), (S[stsem.name], stsem.n)],
                    inc=S["act"])
                pbfree[b] = S["act"].n
            a_stg = S["act"].n
            dma("sync", kt_slice(dmat, csl), kt3(stg, c.TC2)[:, :, :], stsem,
                waits=[(S["act"], a_stg)])
    barrier()

    dma("sync", expc[:, :], d_expc, S["ld"])
    P.wait("scalar", S["ld"], S["ld"].n)
    setlast = [0, 0]
    vt_rd = 0
    th_rd = 0
    o_done = 0
    for h in range(c.H):
        hs = h % 2
        hsem = S["h0"] if hs == 0 else S["h1"]
        dma("sync", qhB[hs][:, :], d_qh[h * c.DH:(h + 1) * c.DH, :], hsem,
            waits=[(S["pe"], setlast[hs])])
        dma("sync", khB[hs][:, :], d_kh[h * c.DH:(h + 1) * c.DH, :], hsem)
        dma("sync", vhB[hs][:, :], d_vh[h * c.DH:(h + 1) * c.DH, :], hsem)
        hv = hsem.n
        dma("gpsimd", th[:, :], d_alibi[h], S["thld"],
            waits=[(S["dve"], th_rd)])
        dma("gpsimd", kt3(dgv, 2 * c.QC)[:, 0:2, :],
            d_alibid[h].rearrange("v p q -> p v q"), S["thld"])
        thv = S["thld"].n
        if h == 0:
            P.op("gpsimd", lambda E: E.memset(
                vtB.rearrange("p (kt e) -> p kt e", e=c.DH + 1)[
                    :, :, c.DH:c.DH + 1], 1.0), inc=S["gpc"])
            gset = S["gpc"].n
        tp_free = 0
        for kt in range(c.KTT):
            P.op("tensor", (lambda kt=kt, hs=hs: (lambda E: E.transpose(
                pbt[:, 0:c.DH], vhB[hs][:, kt * 128:(kt + 1) * 128],
                idn[0:c.DH, 0:c.DH])))(),
                waits=([(S[hsem.name], hv), (S["pe"], vt_rd)] if kt == 0
                       else [(S["dve"], tp_free)]),
                inc=S["pe"])
            pe_t = S["pe"].n
            P.op("vector", (lambda kt=kt: (lambda E: E.tensor_copy(
                vtB[:, kt * (c.DH + 1): kt * (c.DH + 1) + c.DH],
                pbt[:, 0:c.DH])))(),
                waits=[(S["pe"], pe_t)] +
                      ([(S["gpc"], gset)] if (h == 0 and kt == 0) else []),
                inc=S["dve"])
            tp_free = S["dve"].n
        d_vt = S["dve"].n
        sb_free = [0, 0]
        pt_free = [0, 0]
        ps_free = [0, 0]
        pbP = [pbA, pbB]
        for qc in range(c.NQC):
            qt0 = qc * c.QC
            nkt = (qt0 + c.QC) // 128
            npair = nkt // 2
            ndiag_pair = 2
            for pi in range(npair):
                b = pi % 2
                kt0, kt1 = 2 * pi, 2 * pi + 1
                for half, kt in ((0, kt0), (1, kt1)):
                    P.op("tensor", (lambda kt=kt, qt0=qt0, b=b, hs=hs,
                         half=half: (lambda E: E.matmul(
                             pbP[b][:, half * c.QC:(half + 1) * c.QC],
                             khB[hs][:, kt * 128:(kt + 1) * 128],
                             qhB[hs][:, qt0:qt0 + c.QC],
                             start=True, stop=True)))(),
                        waits=([(S["dve"], ps_free[b])] if half == 0 else []) +
                              ([(S[hsem.name], hv)] if (qc == 0 and pi == 0)
                               else []),
                        inc=(S["pe"] if half == 1 else None))
                pe_s = S["pe"].n
                if kt0 * 128 >= qt0:
                    pv = (kt0 * 128 - qt0) // 256
                    btile = dgv[:, pv * 2 * c.QC:(pv + 1) * 2 * c.QC]
                else:
                    btile = th[:, :]
                P.op("vector", (lambda b=b, btile=btile: (lambda E:
                     E.tensor_tensor(
                         sbt[b][:, :], pbP[b][:, :], btile, ALU.add)))(),
                    waits=[(S["pe"], pe_s), (S["act"], sb_free[b])] +
                          ([(S["thld"], thv)] if (qc == 0 and pi == 0) else []),
                    inc=S["dve"])
                ps_free[b] = S["dve"].n
                d_sb = S["dve"].n
                oi = (kt0 * 128 - qt0 + 3584) // 128
                eccol = expc[:, h * 32 + oi: h * 32 + oi + 1]
                P.op("scalar", (lambda b=b, eccol=eccol: (lambda E:
                     E.activation(
                         ptt[b][:, :], sbt[b][:, :], AF.Exp, bias=eccol)))(),
                    waits=[(S["dve"], d_sb), (S["pe"], pt_free[b])],
                    inc=S["act"])
                sb_free[b] = S["act"].n
                a_pt = S["act"].n
                for half, kt in ((0, kt0), (1, kt1)):
                    P.op("tensor", (lambda kt=kt, b=b, nkt=nkt, half=half:
                         (lambda E: E.matmul(
                             pb[4][0:c.DH + 1, 0:c.QC],
                             vtB[:, kt * (c.DH + 1):(kt + 1) * (c.DH + 1)],
                             ptt[b][:, half * c.QC:(half + 1) * c.QC],
                             start=(kt == 0), stop=(kt == nkt - 1))))(),
                        waits=([(S["act"], a_pt)] if half == 0 else []) +
                              ([(S["dve"], d_vt)] if (pi == 0 and half == 0)
                               else []) +
                              ([(S["dve"], o_done)] if (pi == 0 and half == 0)
                               else []),
                        inc=(S["pe"] if kt == nkt - 1 else None))
                pt_free[b] = S["pe"].n
            pe_o = S["pe"].n
            P.op("vector", lambda E: E.reciprocal(
                rrw[0:1, :], pb[4][c.DH:c.DH + 1, 0:c.QC]),
                waits=[(S["pe"], pe_o)], inc=S["dve"])
            P.op("vector", lambda E: E.tensor_copy(rrb[0:1, :], rrw[0:1, :]),
                 waits=[(S["dve"], S["dve"].n)], inc=S["dve"])
            d_rr = S["dve"].n
            P.op("tensor", lambda E: E.matmul(
                pb[5][0:c.DH, 0:c.QC], ones1[0:1, 0:c.DH], rrb[0:1, :],
                start=True, stop=True),
                waits=[(S["dve"], d_rr)], inc=S["pe"])
            pe_rb = S["pe"].n
            P.op("scalar", lambda E: E.activation(
                obuf[:, :], pb[4][0:c.DH, 0:c.QC], AF.Copy),
                waits=[(S["pe"], pe_o)], inc=S["act"])
            a_ob = S["act"].n
            P.op("vector", lambda E: E.tensor_copy(
                rbsb[:, :], pb[5][0:c.DH, 0:c.QC]),
                waits=[(S["pe"], pe_rb)], inc=S["dve"])
            d_rbs = S["dve"].n
            P.op("vector", lambda E: E.tensor_tensor(
                obuf2[:, :], obuf[:, :], rbsb[:, :], ALU.mult),
                waits=[(S["dve"], d_rbs), (S["act"], a_ob),
                       (S["ost"], S["ost"].n)],
                inc=S["dve"])
            o_done = S["dve"].n
            dma("sync", d_oh[h * c.DH:(h + 1) * c.DH, qt0:qt0 + c.QC],
                obuf2[:, :], S["ost"], waits=[(S["dve"], o_done)])
        th_rd = o_done
        setlast[hs] = pe_o
        vt_rd = pe_o
    barrier()

    dma("gpsimd", woW1[:, :], d_wo.rearrange("(kt p) d -> p kt d", p=128),
        S["w1a"])
    wo_ok = S["w1a"].n
    load_ln_tables(want_gate=True, want_battn=True)
    x_rd = 0
    o_rd = 0
    ln_prev = 0
    xch_act = 0
    st_prev = 0
    stats_fr = 0
    pb6_fr = 0
    pbfree5 = [0, 0]
    for ci in range(c.NC2):
        t0 = ci * c.TC2
        csl = slice(t0, t0 + c.TC2)
        ldsem = S["xa"] if ci % 2 == 0 else S["xb"]
        dma("sync", kt3(xchW1, c.TC2)[:, :, :], kt_slice(d_x1, csl), ldsem,
            waits=[(S["dve"], x_rd), (S["act"], xch_act),
                   (S["st"], st_prev)])
        xv = ldsem.n
        dma("sync", kt3(ochW2, c.TC2)[:, :, :], kt_slice(d_oh, csl), S["oc"],
            waits=[(S["pe"], o_rd)])
        ov = S["oc"].n
        dvx2 = 0
        for m in range(c.KT):
            b = m % 2
            for k in range(c.KT):
                w_ = ([(S["oc"], ov), (S["w1a"], wo_ok),
                       (S["dve"], pbfree5[b])] if k == 0 else ())
                P.op("tensor", (lambda k=k, m=m, b=b: (lambda E: E.matmul(
                    pb[b][:, 0:c.TC2],
                    woW1[:, k * c.D + m * 128: k * c.D + m * 128 + 128],
                    ochW2[:, k * c.TC2:(k + 1) * c.TC2],
                    start=(k == 0), stop=(k == c.KT - 1))))(),
                    waits=w_, inc=(S["pe"] if k == c.KT - 1 else None))
            pe_m = S["pe"].n
            o_rd = pe_m
            P.op("vector", (lambda m=m, b=b: (lambda E: E.scalar_tensor_tensor(
                xchW1[:, m * c.TC2:(m + 1) * c.TC2], pb[b][:, 0:c.TC2],
                battn[:, 3 * c.KT + m: 3 * c.KT + m + 1],
                xchW1[:, m * c.TC2:(m + 1) * c.TC2], ALU.add, ALU.add)))(),
                waits=[(S["pe"], pe_m), (S[ldsem.name], xv)], inc=S["dve"])
            dvx2 = S["dve"].n
            pbfree5[b] = dvx2
        x_rd = dvx2
        dma("sync", kt_slice(d_x2, csl), kt3(xchW1, c.TC2)[:, :, :], S["st"],
            waits=[(S["dve"], dvx2)])
        st_prev = S["st"].n
        P.op("scalar", lambda E: E.activation(x1bW2[:, :], xchW1[:, :], AF.Copy),
             waits=[(S["dve"], dvx2), (S["dve"], ln_prev)], inc=S["act"])
        a_b = S["act"].n
        xch_act = a_b
        P.op("scalar", lambda E: E.activation(xsqW2[:, :], x1bW2[:, :],
                                              AF.Square),
             waits=[(S["act"], a_b)], inc=S["act"])
        a_q = S["act"].n
        P.wait("tensor", S["act"], a_q)
        ln_prev, stats_fr, pb6_fr = emit_ln(
            x1bW2, xsqW2, xln, c.T, t0, router=True,
            stats_free=stats_fr, pb6_free=pb6_fr)
    barrier()

    emit_ffn()
    barrier()

    x_rd = 0
    st_prev = 0
    for ci in range(c.NC2):
        t0 = ci * c.TC2
        csl = slice(t0, t0 + c.TC2)
        ldsem = S["xa"] if ci % 2 == 0 else S["xb"]
        dma("sync", kt3(xchP7, c.TC2)[:, :, :], kt_slice(d_x2, csl), ldsem,
            waits=[(S["dve"], x_rd), (S["st"], st_prev)])
        xv = ldsem.n
        P.op("scalar", (lambda t0=t0: (lambda E: E.activation(
            kt3(yfP7, c.TC2)[:, :, :], kt3(ybuf, c.T)[:, :, t0:t0 + c.TC2],
            AF.Identity, scale=0.5)))(),
            waits=[(S["dve"], x_rd)], inc=S["act"])
        a_yf = S["act"].n
        P.op("vector", lambda E: E.tensor_tensor(
            xchP7[:, :], yfP7[:, :], xchP7[:, :], ALU.add),
            waits=[(S["act"], a_yf), (S[ldsem.name], xv)], inc=S["dve"])
        d_o = S["dve"].n
        x_rd = d_o
        dma("sync", kt_slice(d_out, csl), kt3(xchP7, c.TC2)[:, :, :], S["st"],
            waits=[(S["dve"], d_o)])
        st_prev = S["st"].n
    P.wait("sync", S["st"], S["st"].n)

    with nc.Block() as block:
        block.sync(lambda E: P.emit("sync", E))
        block.gpsimd(lambda E: P.emit("gpsimd", E))
        block.tensor(lambda E: P.emit("tensor", E))
        block.vector(lambda E: P.emit("vector", E))
        block.scalar(lambda E: P.emit("scalar", E))

    es.close()
    return nc, list(DI.keys())


def host_constants(c: Cfg):
    slopes = c.slopes
    p = np.arange(128, dtype=np.float32)[:, None]
    qf = np.arange(c.QC, dtype=np.float32)[None, :]
    alibi = np.zeros((c.H, 128, 2 * c.QC), np.float32)
    alibid = np.zeros((c.H, 2, 128, 2 * c.QC), np.float32)
    for hi, s in enumerate(slopes):
        for half in range(2):
            base = np.float32(s) * (128 * half + p - qf)
            alibi[hi, :, half * c.QC:(half + 1) * c.QC] = base
        for pv in range(2):
            for half in range(2):
                o = 128 * (2 * pv + half)
                base = np.float32(s) * (128 * half + p - qf)
                alibid[hi, pv, :, half * c.QC:(half + 1) * c.QC] = \
                    np.where(qf < p + o, np.float32(-1e30), base)
    expc = np.zeros((128, c.H * 32), np.float32)
    for hi, s in enumerate(slopes):
        for oi in range(32):
            off = oi * 128 - 3584
            expc[:, hi * 32 + oi] = np.float32(s * off - c.CC)
    return alibi, alibid, expc


def make_inputs(c: Cfg, full):
    f32 = np.float32
    asc = np.ascontiguousarray
    alibi, alibid, expc = host_constants(c)
    w = {}
    w["wqT"] = asc(np.asarray(full["L0_attn_Wq"]).T.astype(f32))
    w["wkT"] = asc(np.asarray(full["L0_attn_Wk"]).T.astype(f32))
    w["wvT"] = asc(np.asarray(full["L0_attn_Wv"]).T.astype(f32))
    w["woT"] = asc(np.asarray(full["L0_attn_Wo"]).T.astype(f32))
    batt = np.stack([np.asarray(full["L0_attn_bq"]), np.asarray(full["L0_attn_bk"]),
                     np.asarray(full["L0_attn_bv"]), np.asarray(full["L0_attn_bo"])])
    w["battn"] = asc(batt.reshape(4, c.KT, 128).transpose(2, 0, 1)
                     .reshape(128, 4 * c.KT).astype(f32))
    w["gateT"] = asc(np.asarray(full["L0_ffn_gate_w"]).T.astype(f32))
    w["fc1wT"] = asc(np.asarray(full["L0_ffn_fc1_w"]).transpose(0, 2, 1).astype(f32))
    w["fc1b_sb"] = asc(np.asarray(full["L0_ffn_fc1_b"]).T
                       .reshape(2 * c.F // 128, 128, c.NE)
                       .transpose(1, 0, 2).reshape(128, -1).astype(f32))
    w["mult_sb"] = asc(np.asarray(full["L0_ffn_geglu_mult"]).T
                       .reshape(c.F // 128, 128, c.NE)
                       .transpose(1, 0, 2).reshape(128, -1).astype(f32))
    w["fc2wT"] = asc(np.asarray(full["L0_ffn_fc2_w"]).transpose(0, 2, 1).astype(f32))
    w["b2cat"] = asc(np.concatenate(
        [np.asarray(full["L0_ffn_fc2_b"]),
         np.asarray(full["L0_ffn_s_fc2_b"])[None]], 0).astype(f32))
    w["sfc1wT"] = asc(np.asarray(full["L0_ffn_s_fc1_w"]).T.astype(f32))
    w["sfc1b_sb"] = asc(np.asarray(full["L0_ffn_s_fc1_b"])
                        .reshape(2 * c.FS // 128, 128).T.astype(f32))
    w["smult_sb"] = asc(np.asarray(full["L0_ffn_s_geglu_mult"])
                        .reshape(c.FS // 128, 128).T.astype(f32))
    w["sfc2wT"] = asc(np.asarray(full["L0_ffn_s_fc2_w"]).T.astype(f32))
    w["ones1"] = np.ones((1, 128), f32)
    w["onescol"] = np.ones((128, 1), f32)
    w["idn"] = np.eye(128, dtype=f32)
    oneh = np.zeros((c.NE + 1, c.NE * 128), f32)
    for u in range(c.NE):
        oneh[u, u * 128:(u + 1) * 128] = 1.0
    w["onehot"] = oneh
    w["onesT"] = np.ones((1, c.T), f32)
    w["cellsv"] = np.array([[1e-5, 0, 0, 0, 0, 0, 0, 0]], f32)
    w["alibi"] = alibi
    w["alibid"] = alibid
    w["expconst"] = expc
    x = np.asarray(full["L0_attn_x"], dtype=f32)
    maps = []
    for b in range(x.shape[0]):
        m = dict(w)
        m["xT"] = asc(x[b].T)
        maps.append(m)
    return maps


_CFG = Cfg(T=4096, D=1024, F=1024, FS=2048, H=16, NE=15, TC=512, TC2=256)


def kernel(**inputs):
    cfg = _CFG
    maps = make_inputs(cfg, inputs)
    nc, _ = build_program(cfg)
    res = run_bass_kernel_spmd(nc, maps, core_ids=list(range(len(maps))))
    outs = [np.asarray(r["outT"], dtype=np.float32).T for r in res.results]
    return np.stack(outs, axis=0)
